# version 44
# baseline (speedup 1.0000x reference)
import contextlib
import numpy as np

import concourse.bass as bass
import concourse.mybir as mybir
from concourse.bass_utils import run_bass_kernel_spmd

F32 = mybir.dt.float32
BF16 = mybir.dt.bfloat16
U8 = mybir.dt.uint8
AF = mybir.ActivationFunctionType
ALU = mybir.AluOpType
AX = mybir.AxisListType.X

ENGINES = ("tensor", "vector", "scalar", "gpsimd", "sync")

REAL_CFG = dict(NSEQ=2, S=2048, D=1024, NH=16, FF=4096, TT=512)


class Sem:
    __slots__ = ("h", "n", "name")

    def __init__(self, nc, name):
        self.h = nc.alloc_semaphore(name)
        self.n = 0
        self.name = name

    def tick(self, k=1):
        self.n += k
        return self.n


class MK:

    def __init__(self, cfg):
        self.cfg = dict(cfg)
        c = self.cfg
        c["DC"] = c["D"] // 128
        c["QT"] = c["S"] // 128
        c["NT"] = c["NSEQ"] * c["S"] // 128
        c["FC"] = c["FF"] // 128
        c["NTT"] = c["NSEQ"] * c["S"] // c["TT"]
        c["TG"] = c["TT"] // 128
        c["SW"] = min(512, c["S"])
        c["SQ"] = c["S"] // c["SW"]
        c["OW"] = min(512, c["D"])
        c["DH"] = c["D"] // c["OW"]
        assert 64 * c["NH"] == c["D"]
        self.nc = bass.Bass("TRN2", target_bir_lowering=False)
        self.q = {e: [] for e in ENGINES}
        self.es = {e: Sem(self.nc, f"es_{e}") for e in ENGINES}
        self.bar = Sem(self.nc, "bar")
        self._sems = {}
        self.ident = None

    def sem(self, name):
        if name not in self._sems:
            self._sems[name] = Sem(self.nc, name)
        return self._sems[name]

    def do(self, eng, fn, waits=(), inc=None, inck=1):
        for s, v in waits:
            if v:
                self.q[eng].append(lambda E, h=s.h, v=v: E.wait_ge(h, v))
        tick = None
        if inc is not None:
            tick = inc.tick(inck)
            self.q[eng].append(
                lambda E, fn=fn, h=inc.h, k=inck: fn(E).then_inc(h, k))
        else:
            self.q[eng].append(fn)
        return tick

    def ed(self, eng, fn, waits=()):
        return self.do(eng, fn, waits=waits, inc=self.es[eng])

    def flush(self):
        qs, self.q = self.q, {e: [] for e in ENGINES}
        with self.nc.Block() as b:
            for e in ENGINES:
                if qs[e]:
                    getattr(b, e)(lambda E, fns=qs[e]: [f(E) for f in fns])

    def barrier(self):
        for e in ENGINES:
            self.do(e, lambda E: E.nop(), inc=self.bar)
        v = self.bar.n
        for e in ENGINES:
            self.do(e, lambda E, h=self.bar.h, v=v: E.wait_ge(h, v))

    def build(self):
        c, nc = self.cfg, self.nc
        NSEQ, S, D, FF = c["NSEQ"], c["S"], c["D"], c["FF"]
        NT = c["NT"]

        xin = nc.dram_tensor("xin", [NSEQ * S, D], F32, kind="ExternalInput")
        wq = nc.dram_tensor("wq", [D, D], F32, kind="ExternalInput")
        wk = nc.dram_tensor("wk", [D, 64], F32, kind="ExternalInput")
        wv = nc.dram_tensor("wv", [D, 64], F32, kind="ExternalInput")
        bq = nc.dram_tensor("bq", [D, 64], F32, kind="ExternalInput")
        bk = nc.dram_tensor("bk", [D, 64], F32, kind="ExternalInput")
        vo = nc.dram_tensor("vo", [D, D], F32, kind="ExternalInput")
        k0 = nc.dram_tensor("k0", [2 * D, FF], U8, kind="ExternalInput")
        k1 = nc.dram_tensor("k1", [FF, FF], U8, kind="ExternalInput")
        k2 = nc.dram_tensor("k2", [FF, D], U8, kind="ExternalInput")
        s0 = nc.dram_tensor("s0", [1, 1], F32, kind="ExternalInput")
        s1 = nc.dram_tensor("s1", [1, 1], F32, kind="ExternalInput")
        s2 = nc.dram_tensor("s2", [1, 1], F32, kind="ExternalInput")
        y = nc.dram_tensor("y", [NSEQ * S, D], F32, kind="ExternalOutput")
        w0s = nc.dram_tensor("w0s", [D, FF], BF16, kind="Internal")
        w1s = nc.dram_tensor("w1s", [FF, FF], BF16, kind="Internal")
        w2s = nc.dram_tensor("w2s", [FF, D], BF16, kind="Internal")

        with contextlib.ExitStack() as top:
            en = top.enter_context
            x_sb = en(nc.sbuf_tensor("x_sb", [128, NT * D], BF16))
            self.ident = en(nc.sbuf_tensor("ident", [128, 128], BF16))
            ones_row = en(nc.sbuf_tensor("ones_row", [1, 128], F32))
            self.ones_row = ones_row
            NH = c["NH"]
            DC = c["DC"]
            self.ones16 = en(nc.sbuf_tensor("ones16", [128, DC * NH],
                                            BF16))
            s_sb = en(nc.sbuf_tensor("s_sb", [1, 3], F32))
            scb = en(nc.sbuf_tensor("scb", [128, 9], F32))
            self.lnst = en(nc.sbuf_tensor("lnst", [128, 128], F32))
            self.negmax = en(nc.sbuf_tensor("negmax", [128, 2], F32))
            self.negmaxq = en(nc.sbuf_tensor("negmaxq",
                                             [128, 2 * c["SQ"]], F32))
            self.biasb = en(nc.sbuf_tensor("biasb", [128, 2], F32))
            self.sump = en(nc.sbuf_tensor("sump", [128, 2 * c["SQ"]], F32))
            self.ssum = en(nc.sbuf_tensor("ssum", [128, 2], F32))
            self.recb = en(nc.sbuf_tensor("recb", [128, 4], F32))

            dxs = [self.sem(f"dx{i}") for i in range(NSEQ)]
            self.init_consts(ones_row)
            self.load_x(xin, x_sb, dxs)
            self.scalars_bcast(s0, s1, s2, s_sb, scb, ones_row)
            for d in dxs:
                self.do("gpsimd", lambda E, d=d: E.nop(), waits=[(d, d.n)])
            self.flush()

            self._pre = None
            hq0 = en(nc.sbuf_tensor("hq0", [128, D], BF16))
            self._hq0 = hq0
            hoist_sq = [hq0, hq0]
            TG = c["TG"]

            def hoist(tiles, n):
                lv, la = self.ln_stats(tiles, x_sb, hoist_sq,
                                       w_first=[(self.es["vector"],
                                                 self.es["vector"].n),
                                                (self.es["scalar"],
                                                 self.es["scalar"].n)],
                                       sb=64)
                self._pre = self.ln_alg(n, lv, la, sb=64)

            with contextlib.ExitStack() as pps:
                pen = pps.enter_context
                QWp = min(1024, FF)
                pp_tb = [pen(nc.sbuf_tensor(f"pp_t{i}", [128, QWp], BF16))
                         for i in range(3)]
                pp_bb = [pen(nc.sbuf_tensor(f"pp_b{i}", [128, QWp], BF16))
                         for i in range(2)]
                pp = lambda: self.prepass(k0, k1, k2, scb, w0s, w1s, w2s,
                                          (pp_tb, pp_bb))
                self.attn_a(x_sb, wq, wk, wv, dxs, post=pp,
                            end_hook=lambda: hoist(list(range(TG)), TG))
                self.ffn(x_sb, w0s, w1s, w2s, None, hooks={})
            self.attn_b(x_sb, bq, bk, vo,
                        end_hook=lambda: hoist(list(range(TG)), TG))
            self.ffn(x_sb, w0s, w1s, w2s, y, hooks={})
            self.flush()
        return nc

    def init_consts(self, ones_row):
        c = self.cfg
        P = self.es["gpsimd"]
        self.do("gpsimd", lambda E: E.memset(ones_row[:], 1.0), inc=P)
        t = self.do("gpsimd", lambda E: E.memset(self.ident[:], 1.0), inc=P)
        self.do("gpsimd", lambda E: E.affine_select(
            out=self.ident[:], in_=self.ident[:], compare_op=ALU.is_equal,
            fill=0.0, base=0, pattern=[[-1, 128]], channel_multiplier=1),
            waits=[(P, t)], inc=P)
        NH = c["NH"]
        tz = self.do("gpsimd", lambda E: E.memset(self.ones16[:], 0.0),
                     inc=P)
        pv = tz
        for cdim in range(c["DC"]):
            pv = self.do("gpsimd", lambda E, cdim=cdim, NH=NH: E.memset(
                self.ones16[0:64, NH * cdim + 2 * cdim:
                            NH * cdim + 2 * cdim + 1], 1.0),
                waits=[(P, pv)], inc=P)
            pv = self.do("gpsimd", lambda E, cdim=cdim, NH=NH: E.memset(
                self.ones16[64:128, NH * cdim + 2 * cdim + 1:
                            NH * cdim + 2 * cdim + 2], 1.0),
                waits=[(P, pv)], inc=P)
        self.ones16_tick = pv

    def load_x(self, xin, x_sb, dxs):
        c = self.cfg
        QT = c["QT"]
        for t in range(c["NT"]):
            self.do("gpsimd", lambda E, t=t: E.dma_start(
                x_sb[:, c["D"] * t:c["D"] * (t + 1)],
                xin[128 * t:128 * (t + 1), :]), inc=dxs[t // QT], inck=16)

    def scalars_bcast(self, s0, s1, s2, s_sb, scb, ones_row):
        nc = self.nc
        G, V, T = self.es["gpsimd"], self.es["vector"], self.es["tensor"]
        ds = self.sem("d_scal")
        self._scb_zero = self.do("gpsimd", lambda E: E.memset(
            scb[:, 8:9], 0.0), inc=self.es["gpsimd"])
        with nc.psum_tensor("pp_ps", [128, 1], F32) as psc:
            for j, sx in enumerate((s0, s1, s2)):
                self.do("gpsimd", lambda E, sx=sx, j=j: E.dma_start(
                    s_sb[:, j:j + 1], sx[:]), inc=ds, inck=16)
            tmul = [0, 0, 0]
            prev_v = 0
            for j in range(3):
                w = [(ds, ds.n), (G, G.n)] if j == 0 else [(V, prev_v)]
                t = self.ed("tensor", lambda E, j=j: E.matmul(
                    psc[:], ones_row[:], s_sb[:, j:j + 1],
                    start=True, stop=True), waits=w)
                prev_v = self.ed("vector", lambda E, j=j: E.tensor_copy(
                    scb[:, j:j + 1], psc[:]), waits=[(T, t)])
                tmul[j] = prev_v
            vt = 0
            for dst, srcc, mul in ((3, 0, 2.0), (4, 1, 4.0), (5, 1, -2.0),
                                   (6, 2, 4.0), (7, 2, -2.0)):
                vt = self.ed("vector", lambda E, d=dst, sr=srcc, m=mul:
                             E.tensor_scalar_mul(
                    scb[:, d:d + 1], scb[:, sr:sr + 1], m),
                    waits=[(V, tmul[srcc])])
            self.scal_tick = vt

    def prepass(self, k0, k1, k2, scb, w0s, w1s, w2s, bufs):
        c, nc = self.cfg, self.nc
        D, FF, DC, FC = c["D"], c["FF"], c["DC"], c["FC"]
        G = self.es["gpsimd"]
        V = self.es["vector"]
        QW = min(1024, FF)
        NQ = FF // QW
        tb, bb = bufs
        R = len(tb)
        din = [self.sem(f"d_pp{i}") for i in range(R)]
        dout = [self.sem(f"d_ppo{i}") for i in range(R)]
        free = {i: 0 for i in range(R)}
        stored = {i: 0 for i in range(R)}

        specs = []
        for ci in range(DC):
            for q in range(NQ):
                bsl = len(specs) % len(bb)
                specs.append((
                    (k0[128 * ci:128 * (ci + 1), QW * q:QW * (q + 1)],
                     k0[128 * (ci + DC):128 * (ci + DC + 1),
                        QW * q:QW * (q + 1)]),
                    w0s[128 * ci:128 * (ci + 1), QW * q:QW * (q + 1)],
                    [lambda E, ib, bsl=bsl: E.tensor_sub(
                        tb[ib][:], tb[ib][:], bb[bsl][:]),
                     lambda E, ib: E.tensor_scalar(
                         tb[ib][:], tb[ib][:], scb[:, 3:4], scb[:, 8:9],
                         op0=ALU.mult, op1=ALU.add)], QW))
        n_k0 = len(specs)
        for ci in range(FC):
            for q in range(NQ):
                specs.append((
                    (k1[128 * ci:128 * (ci + 1), QW * q:QW * (q + 1)],
                     None),
                    w1s[128 * ci:128 * (ci + 1), QW * q:QW * (q + 1)],
                    [lambda E, ib: E.tensor_scalar(
                        tb[ib][:], tb[ib][:], scb[:, 4:5], scb[:, 5:6],
                        op0=ALU.mult, op1=ALU.add)], QW))
        n_k1 = len(specs)
        NQ2 = max(1, D // QW)
        QW2 = min(QW, D)
        for ci in range(FC):
            for q in range(NQ2):
                specs.append((
                    (k2[128 * ci:128 * (ci + 1),
                        QW2 * q:QW2 * (q + 1)], None),
                    w2s[128 * ci:128 * (ci + 1), QW2 * q:QW2 * (q + 1)],
                    [lambda E, ib: E.tensor_scalar(
                        tb[ib][:, 0:QW2], tb[ib][:, 0:QW2],
                        scb[:, 6:7], scb[:, 7:8],
                        op0=ALU.mult, op1=ALU.add)], QW2))

        def emit_load(si):
            ib = si % R
            srcs, _, _, w = specs[si]
            for j, (t_, src) in enumerate(zip((tb, bb), srcs)):
                if src is None:
                    continue
                bi = si % len(t_)
                self.do("gpsimd", lambda E, t_=t_, src=src, bi=bi, w=w:
                        E.dma_start(t_[bi][:, 0:w], src),
                        waits=(([(G, free[ib])] if free[ib] else [])
                               + ([(dout[ib], stored[ib])]
                                  if stored[ib] else [])) if j == 0
                              else [], inc=din[ib], inck=16)

        N = len(specs)
        LEAD = min(2, N)
        for si in range(LEAD):
            emit_load(si)
        for si in range(N):
            _, dst, ops, w = specs[si]
            ib = si % R
            g = 0
            for fn in ops:
                g = self.do("gpsimd", lambda E, fn=fn, ib=ib: fn(E, ib),
                            waits=([(din[ib], din[ib].n),
                                    (V, self.scal_tick),
                                    (G, self._scb_zero)] if g == 0
                                   else [(G, g)]), inc=G)
            free[ib] = g
            self.do("gpsimd", lambda E, dst=dst, ib=ib, w=w:
                    E.dma_start(dst, tb[ib][:, 0:w]), waits=[(G, g)],
                    inc=dout[ib], inck=16)
            stored[ib] = dout[ib].n
            if si + LEAD < N:
                emit_load(si + LEAD)
            if si == n_k0 - 1:
                self._pp_w0 = [(d, d.n) for d in dout]
            if si == n_k1 - 1:
                self._pp_w1 = [(d, d.n) for d in dout]
        self._pp_w2 = [(d, d.n) for d in dout]

    def ln_stats(self, tiles, x_sb, sq_tmp, w_first=(), sb=0):
        c = self.cfg
        D = c["D"]
        lv = la = 0
        ring = 1 if sq_tmp[0] is sq_tmp[1] else 2
        lastsq = {0: 0, 1: 0}
        for j, t in enumerate(tiles):
            xt = x_sb[:, D * t:D * (t + 1)]
            lv = self.ed("vector", lambda E, xt=xt, j=j: E.reduce_sum(
                self.lnst[:, sb + j:sb + j + 1], xt, axis=AX),
                waits=list(w_first) if j == 0 else ())
            la = self.ed("scalar", lambda E, xt=xt, j=j: E.activation(
                sq_tmp[j % ring][:], xt, AF.Square,
                accum_out=self.lnst[:, sb + 16 + j:sb + 17 + j]),
                waits=(list(w_first) if j == 0 else [])
                      + ([(self.es["scalar"], lastsq[j % ring])]
                         if lastsq[j % ring] else []))
            lastsq[j % ring] = la
        return lv, la

    def ln_alg(self, n, lv, la, sb=0):
        c = self.cfg
        D = c["D"]
        V, A = self.es["vector"], self.es["scalar"]
        st = self.lnst
        m = st[:, sb:sb + n]
        q = st[:, sb + 16:sb + 16 + n]
        r = st[:, sb + 32:sb + 32 + n]
        nm = st[:, sb + 48:sb + 48 + n]
        v1 = self.ed("vector", lambda E: E.tensor_scalar_mul(m, m, 1.0 / D),
                     waits=[(V, lv)])
        v2 = self.ed("vector", lambda E: E.tensor_scalar(
            q, q, 1.0 / D, 1e-5, op0=ALU.mult, op1=ALU.add),
            waits=[(A, la), (V, v1)])
        v3 = self.ed("vector", lambda E: E.scalar_tensor_tensor(
            r, m, -1.0, m, op0=ALU.mult, op1=ALU.mult), waits=[(V, v2)])
        v4 = self.ed("vector", lambda E: E.tensor_add(q, q, r),
                     waits=[(V, v3)])
        a5 = self.ed("scalar", lambda E: E.activation(r, q, AF.Sqrt),
                     waits=[(V, v4)])
        v6 = self.ed("vector", lambda E: E.reciprocal(r, r),
                     waits=[(A, a5)])
        v7 = self.ed("vector", lambda E: E.scalar_tensor_tensor(
            nm, m, -1.0, r, op0=ALU.mult, op1=ALU.mult), waits=[(V, v6)])
        return v7

    def ln_norm_tr(self, t, j, x_sb, ln_tmp, lnps_ap, lnT, W, toff,
                   alg_tick, prev_tt=0, w_tr=(), w_a4=(), sb=0):
        c = self.cfg
        D, DC = c["D"], c["DC"]
        V, A, T = self.es["vector"], self.es["scalar"], self.es["tensor"]
        st = self.lnst
        xt = x_sb[:, D * t:D * (t + 1)]
        buf = ln_tmp[j % 2]
        vn = self.ed("vector", lambda E, xt=xt, j=j, buf=buf:
                     E.tensor_scalar(
            buf[:], xt, st[:, sb + 32 + j:sb + 33 + j],
            st[:, sb + 48 + j:sb + 49 + j],
            op0=ALU.mult, op1=ALU.add),
            waits=[(V, alg_tick)] + ([(T, prev_tt)] if prev_tt else []))
        tt = None
        for jj in range(DC):
            tt = self.do("tensor", lambda E, jj=jj, buf=buf: E.transpose(
                lnps_ap[:, 128 * jj:128 * (jj + 1)],
                buf[:, 128 * jj:128 * (jj + 1)], self.ident[:]),
                waits=([(V, vn)] + list(w_tr)) if jj == 0 else (),
                inc=T if jj == DC - 1 else None)
        dst = lnT.rearrange("p (j w) -> p j w", j=DC)[:, :, toff:toff + 128]
        src = lnps_ap[:, 0:DC * 128].rearrange("p (j w) -> p j w", j=DC)
        a4 = self.ed("scalar", lambda E, dst=dst, src=src: E.copy(dst, src),
                     waits=[(T, tt)] + list(w_a4))
        return vn, a4, tt

    def attn_iters(self, NIT, emit_scores, emit_att_info, x_slices,
                   probs, probsT, sc, psT, att_read_ap, scale,
                   first_waits, rec_src=None, split_evac=False,
                   bias_src=None, bias_tick=0):
        c = self.cfg
        S, QT, SQ, SW = c["S"], c["QT"], c["SQ"], c["SW"]
        V, A, T = self.es["vector"], self.es["scalar"], self.es["tensor"]
        t_sc, t_exp4, t_evac, t_tr, t_att, t_stt = {}, {}, {}, {}, {}, {}
        t_bias, t_recp, t_expq, t_trh, t_evA = {}, {}, {}, {}, {}
        self._t_stt_cur = t_stt

        def evw(tp):
            ea, ev = tp
            return ([(A, ea)] if ea else []) + ([(V, ev)] if ev else [])

        for i in range(NIT + 2):
            if i < NIT:
                if i < 1:
                    wq = [[] for _ in range(SQ)]
                else:
                    wq = [[(A, t_expq[i - 1][cq])] for cq in range(SQ)]
                wq[0] = wq[0] + list(first_waits.get(i, []))
                t_sc[i] = {}
                for cq in range(SQ):
                    t_sc[i][cq] = emit_scores(i, cq, wq[cq])
            if i >= 2:
                j = i - 2
                if split_evac:
                    w = [(V, t_evA[j])]
                    wmid = [(V, t_evac[j][1])]
                else:
                    w = evw(t_evac[j])
                    wmid = ()
                if j >= 2:
                    w = w + [(V, t_stt[j - 2])]
                t_att[j] = emit_att_info(j, w, wmid)
            if 1 <= i <= NIT:
                j = i - 1
                jb = j % 2
                if j >= 1:
                    w = evw(t_evac[j - 1]) + [(A, t_exp4[j])]
                else:
                    w = [(A, t_exp4[0])]
                HQ = QT // 2
                for cc in range(QT):
                    half_tick = split_evac and cc == HQ - 1
                    tick = self.do(
                        "tensor", lambda E, jb=jb, cc=cc:
                        E.transpose(
                            psT[:, 128 * cc:128 * (cc + 1)],
                            probs[:, S * jb + 128 * cc:
                                  S * jb + 128 * (cc + 1)],
                            self.ident[:]),
                        waits=w if cc == 0 else (),
                        inc=T if (cc == QT - 1 or half_tick) else None)
                    if half_tick:
                        t_trh[j] = tick
                    if cc == QT - 1:
                        t_tr[j] = tick
            if i < NIT and bias_src is None:
                ib = i % 2
                t_max = 0
                for cq in range(SQ):
                    wq_ = [(T, t_sc[i][cq])]
                    if cq == 0 and i >= 2:
                        wq_.append((V, t_bias[i - 2]))
                    t_max = self.ed("vector", lambda E, ib=ib, cq=cq:
                                    E.reduce_max(
                        self.negmaxq[:, SQ * ib + cq:SQ * ib + cq + 1],
                        sc[:, SW * cq:SW * (cq + 1)], axis=AX,
                        negate=True),
                        waits=wq_)
                t_bias[i] = self.ed("vector", lambda E, ib=ib:
                                    E.tensor_reduce(
                    self.biasb[:, ib:ib + 1],
                    self.negmaxq[:, SQ * ib:SQ * (ib + 1)], AX, ALU.min),
                    waits=[(V, t_max)])
            if i >= 2:
                j = i - 2
                jb = j % 2
                if rec_src is not None:
                    t_recp[j] = self.ed("vector",
                                        lambda E, jb=jb, a=rec_src(j):
                                        E.reciprocal(
                        self.recb[:, jb:jb + 1], a),
                        waits=[(T, t_att[j])]
                              + ([(V, t_stt[j - 2])] if j >= 2 else []))
                w = [(T, t_att[j]), (V, t_recp[j])]
                for xsl, asrc in x_slices(j):
                    t_stt[j] = self.ed("vector",
                                       lambda E, xsl=xsl, asrc=asrc, jb=jb:
                                       E.scalar_tensor_tensor(
                        xsl, asrc, self.recb[:, jb:jb + 1], xsl,
                        op0=ALU.mult, op1=ALU.add), waits=w)
                    w = []
            if i < NIT:
                ib = i % 2
                bias_ap = (self.biasb[:, ib:ib + 1] if bias_src is None
                           else bias_src(i))
                for cq in range(SQ):
                    w = []
                    if cq == 0:
                        if bias_src is None:
                            w = [(V, t_bias[i])]
                        elif i <= 1 and bias_tick:
                            w = [(V, bias_tick)]
                        if i >= 2:
                            w.append((T, t_tr[i - 2]))
                    if bias_src is not None:
                        w = w + [(T, t_sc[i][cq])]
                    acc = (None if rec_src is not None else
                           self.sump[:, SQ * ib + cq:SQ * ib + cq + 1])
                    t_exp4[i] = self.ed("scalar", lambda E, ib=ib, cq=cq,
                                        acc=acc, bias_ap=bias_ap:
                                        E.activation(
                        probs[:, S * ib + SW * cq:S * ib + SW * (cq + 1)],
                        sc[:, SW * cq:SW * (cq + 1)], AF.Exp,
                        bias=bias_ap, scale=1.0,
                        accum_out=acc),
                        waits=w)
                    t_expq.setdefault(i, {})[cq] = t_exp4[i]
            if 1 <= i <= NIT:
                j = i - 1
                jb = j % 2
                war = [(T, t_att[j - 2])] if j >= 2 else []
                if split_evac:
                    H = S // 2
                    t_evA[j] = self.ed(
                        "vector", lambda E, jb=jb, H=H:
                        E.tensor_copy(
                            probsT[:, S * jb:S * jb + H], psT[:, 0:H]),
                        waits=[(T, t_trh[j])] + war)
                    tv = self.ed("vector", lambda E, jb=jb, H=H:
                                 E.tensor_copy(
                        probsT[:, S * jb + H:S * (jb + 1)], psT[:, H:S]),
                        waits=[(T, t_tr[j])])
                    t_evac[j] = (0, tv)
                else:
                    t_evac[j] = (self.ed("scalar", lambda E, jb=jb: E.copy(
                        probsT[:, S * jb:S * (jb + 1)], psT[:]),
                        waits=[(T, t_tr[j])] + war), 0)
            if rec_src is None:
                if i < NIT:
                    ib = i % 2
                    t_sum = self.ed("vector", lambda E, ib=ib: E.reduce_sum(
                        self.ssum[:, ib:ib + 1],
                        self.sump[:, SQ * ib:SQ * (ib + 1)], axis=AX),
                        waits=[(A, t_exp4[i])]
                              + ([(V, t_recp[i - 2])] if i >= 2 else []))
                    t_recp[i] = self.ed("vector", lambda E, ib=ib:
                                        E.reciprocal(
                        self.recb[:, ib:ib + 1], self.ssum[:, ib:ib + 1]),
                        waits=[(V, t_sum)]
                              + ([(V, t_stt[i - 2])] if i >= 2 else []))
        return t_stt

    def attn_iters_t(self, NIT, emit_scores, emit_att_info, x_slices,
                     probsT, sc, rec_src, bias_src, first_waits):
        c = self.cfg
        S, QT = c["S"], c["QT"]
        V, A, T = self.es["vector"], self.es["scalar"], self.es["tensor"]
        EW = min(1024, S)
        NE = S // EW
        TPE = EW // 128
        t_sc, t_exp, t_att, t_stt, t_recp = {}, {}, {}, {}, {}
        self._t_stt_cur = t_stt
        for i in range(NIT + 2):
            if i < NIT:
                w0 = list(first_waits.get(i, []))
                t_sc[i] = {}
                for c2 in range(QT):
                    w_ = []
                    if i >= 1 and c2 % TPE == 0:
                        w_ = [(A, t_exp[i - 1][c2 // TPE])]
                    if c2 == 0:
                        w_ = w_ + w0
                    tick = emit_scores(i, c2, w_)
                    if (c2 + 1) % TPE == 0:
                        t_sc[i][(c2 + 1) // TPE - 1] = tick
            if i >= 1 and i - 1 < NIT:
                j = i - 1
                w = [(A, t_exp[j][max(0, NE // 2 - 1)])]
                if j >= 2:
                    w.append((V, t_stt[j - 2]))
                wmid = [(A, t_exp[j][NE - 1])]
                t_att[j] = emit_att_info(j, w, wmid)
            if i < NIT:
                jb = i % 2
                for hb in range(NE):
                    w = [(T, t_sc[i][hb])]
                    if hb == 0 and i >= 2:
                        w.append((T, t_att[i - 2]))
                    t_exp.setdefault(i, {})[hb] = self.ed(
                        "scalar", lambda E, jb=jb, hb=hb, b=bias_src(i):
                        E.activation(
                            probsT[:, S * jb + EW * hb:
                                   S * jb + EW * (hb + 1)],
                            sc[:, EW * hb:EW * (hb + 1)], AF.Exp,
                            bias=b, scale=1.0), waits=w)
            if i >= 2:
                j = i - 2
                jb = j % 2
                t_recp[j] = self.ed(
                    "vector", lambda E, jb=jb, a=rec_src(j):
                    E.reciprocal(self.recb[:, jb:jb + 1], a),
                    waits=[(T, t_att[j])]
                          + ([(V, t_stt[j - 2])] if j >= 2 else []))
                w = [(T, t_att[j]), (V, t_recp[j])]
                for xsl, asrc in x_slices(j):
                    t_stt[j] = self.ed(
                        "vector", lambda E, xsl=xsl, asrc=asrc, jb=jb:
                        E.scalar_tensor_tensor(
                            xsl, asrc, self.recb[:, jb:jb + 1], xsl,
                            op0=ALU.mult, op1=ALU.add), waits=w)
                    w = []
        return t_stt

    def attn_iters_t2(self, NIT, emit_scores, emit_att_info, x_slices,
                      probsT, sc, rec_src, bias_src, first_waits):
        c = self.cfg
        S, QT = c["S"], c["QT"]
        V, A, T = self.es["vector"], self.es["scalar"], self.es["tensor"]
        NTP = max(1, QT // 4)
        PHW = 256 * NTP
        NPH = QT // NTP
        t_sc, t_exp, t_att, t_stt, t_recp = {}, {}, {}, {}, {}

        def phase(i, p):
            w_exp = None
            if p >= 2:
                w_exp = t_exp[i][p - 2]
            elif i >= 1:
                w_exp = t_exp[i - 1][p + NPH - 2]
            for ct in range(NTP):
                w_ = []
                if ct == 0 and w_exp is not None:
                    w_ = [(A, w_exp)]
                if p == 0 and ct == 0:
                    w_ = w_ + list(first_waits.get(i, []))
                tick = emit_scores(i, p, ct, w_)
            t_sc[i][p] = tick
            jb = i % 2
            w = [(T, t_sc[i][p])]
            if p == 0 and i >= 2:
                w.append((T, t_att[i - 2][1]))
            t_exp.setdefault(i, {})[p] = self.ed(
                "scalar", lambda E, jb=jb, p=p, b=bias_src(i):
                E.activation(
                    probsT[:, 2 * S * jb + PHW * p:
                           2 * S * jb + PHW * (p + 1)],
                    sc[:, PHW * (p % 2):PHW * (p % 2 + 1)], AF.Exp,
                    bias=b, scale=1.0), waits=w)

        for i in range(NIT + 2):
            if i < NIT:
                t_sc[i] = {}
                phase(i, 0)
                phase(i, 1)
            if i >= 2:
                j = i - 2
                for qh in range(2):
                    u = 2 * j + qh
                    t_recp[u] = self.ed(
                        "vector", lambda E, qh=qh, a=rec_src(j, qh):
                        E.reciprocal(self.recb[:, qh:qh + 1], a),
                        waits=[(T, t_att[j][qh])]
                              + ([(V, t_stt[u - 2])] if u - 2 in t_stt
                                 else []))
                    w = [(T, t_att[j][qh]), (V, t_recp[u])]
                    for xsl, asrc in x_slices(j, qh):
                        t_stt[u] = self.ed(
                            "vector",
                            lambda E, xsl=xsl, asrc=asrc, qh=qh:
                            E.scalar_tensor_tensor(
                                xsl, asrc, self.recb[:, qh:qh + 1], xsl,
                                op0=ALU.mult, op1=ALU.add), waits=w)
                        w = []
            if 1 <= i <= NIT:
                j = i - 1
                t_att[j] = {}
                for qh in range(2):
                    u = 2 * j + qh
                    w = [(A, t_exp[j][NPH - 1])] if qh == 0 else []
                    if u - 2 in t_stt:
                        w = w + [(V, t_stt[u - 2])]
                    t_att[j][qh] = emit_att_info(j, qh, w)
            if i < NIT:
                phase(i, 2)
                phase(i, 3)
        return t_stt

    def attn_iters_t3(self, NIT, emit_scores, emit_att_ph, x_slices,
                      probsT, sc, rec_src, bias_src, first_waits):
        c = self.cfg
        S, QT = c["S"], c["QT"]
        V, A, T = self.es["vector"], self.es["scalar"], self.es["tensor"]
        NTP = max(1, QT // 8)
        NPH = QT // NTP
        PHW = 512 * NTP
        NQH = 4
        t_sc, t_exp, t_aph, t_att = {}, {}, {}, {}
        t_stt, t_recp = {}, {}

        def emit_recstt(j):
            for qh in range(NQH):
                t_recp[(j, qh)] = self.ed(
                    "vector", lambda E, qh=qh, a=rec_src(j, qh):
                    E.reciprocal(self.recb[:, qh:qh + 1], a),
                    waits=[(T, t_att[j])]
                          + ([(V, t_stt[(j - 1, qh)])]
                             if (j - 1, qh) in t_stt else []))
                w = [(T, t_att[j]), (V, t_recp[(j, qh)])]
                for xsl, asrc in x_slices(j, qh):
                    t_stt[(j, qh)] = self.ed(
                        "vector", lambda E, xsl=xsl, asrc=asrc, qh=qh:
                        E.scalar_tensor_tensor(
                            xsl, asrc, self.recb[:, qh:qh + 1], xsl,
                            op0=ALU.mult, op1=ALU.add), waits=w)
                    w = []

        LAG = 2
        for gp in range(NIT * NPH + LAG + 1):
            g, p = divmod(gp, NPH)
            if p == LAG and 1 <= g <= NIT:
                emit_recstt(g - 1)
            if g < NIT:
                if p == 0:
                    t_sc[g], t_aph[g] = {}, {}
                    t_exp[g] = {}
                if p >= 2:
                    wex = t_exp[g].get(p - 2)
                elif g >= 1:
                    wex = t_exp[g - 1][p + NPH - 2]
                else:
                    wex = None
                tick = 0
                for ct in range(NTP):
                    w_ = []
                    if ct == 0 and wex is not None:
                        w_ = [(A, wex)]
                    if p == 0 and ct == 0:
                        w_ = w_ + list(first_waits.get(g, []))
                    tick = emit_scores(g, p, ct, w_)
                t_sc[g][p] = tick
                w = [(T, t_sc[g][p])]
                if p >= 4:
                    war = t_aph[g].get(p - 4)
                elif g >= 1:
                    war = t_aph[g - 1][p + NPH - 4]
                else:
                    war = None
                if war is not None:
                    w.append((T, war))
                t_exp[g][p] = self.ed(
                    "scalar", lambda E, p=p, b=bias_src(g): E.activation(
                        probsT[:, PHW * (p % 4):PHW * (p % 4 + 1)],
                        sc[:, PHW * (p % 2):PHW * (p % 2 + 1)],
                        AF.Exp, bias=b, scale=1.0), waits=w)
            g2, p2 = divmod(gp - LAG, NPH)
            if 0 <= g2 < NIT and gp >= LAG and p2 % 2 == 1:
                for pp in (p2 - 1, p2):
                    sttw = {qh: ([(V, t_stt[(g2 - 1, qh)])]
                                 if (g2 - 1, qh) in t_stt else [])
                            for qh in range(NQH)} if pp == 0 else None
                    t_aph[g2][pp] = emit_att_ph(
                        g2, pp, [(A, t_exp[g2][pp])], sttw)
                    if pp == NPH - 1:
                        t_att[g2] = t_aph[g2][pp]
        return t_stt

    def attn_a(self, x_sb, wq, wk, wv, dxs, post=None, end_hook=None):
        c, nc = self.cfg, self.nc
        D, S, DC, QT, NH = c["D"], c["S"], c["DC"], c["QT"], c["NH"]
        SQ, SW, NSEQ = c["SQ"], c["SW"], c["NSEQ"]
        V, A, T = self.es["vector"], self.es["scalar"], self.es["tensor"]
        assert NSEQ == 2
        dw = self.sem("dw")
        with contextlib.ExitStack() as st:
            en = st.enter_context
            qT = [en(nc.sbuf_tensor(f"qT{q}", [128, DC * S], BF16))
                  for q in range(2)]
            kT = [en(nc.sbuf_tensor(f"kTd{q}", [128, S], BF16))
                  for q in range(2)]
            v_sb = [en(nc.sbuf_tensor(f"v_sb{q}", [128, QT * 65], BF16))
                    for q in range(2)]
            wkv_sb = en(nc.sbuf_tensor("wkv", [128, DC * 128], BF16))
            ln_tmp = [en(nc.sbuf_tensor(f"lntmpA{i}", [128, D], BF16))
                      for i in range(2)]
            biasK = en(nc.sbuf_tensor("biasK", [128, 2 * NH], F32))
            proj_fin = {}
            fin_tick = 0
            with contextlib.ExitStack() as st1:
                en1 = st1.enter_context
                lnT = en1(nc.sbuf_tensor("lnT_A", [128, DC * S], BF16))
                wqr = [en1(nc.sbuf_tensor(f"wqr{i}", [128, DC * 128], BF16))
                       for i in range(2)]
                lnps = en1(nc.psum_tensor("lnpsA", [128, DC * 128], BF16))
                qps = [en1(nc.psum_tensor(f"qpsA{i}", [128, SW], F32))
                       for i in range(2)]
                kps = en1(nc.psum_tensor("kpsA", [128, 64], F32))
                sq2 = en1(nc.sbuf_tensor("sq2", [128, S], BF16))
                qn2 = en1(nc.sbuf_tensor("qn2", [NH, S], BF16))
                nqps = [en1(nc.psum_tensor(f"nqps{i}", [NH, SW], F32))
                        for i in range(SQ)]
                qn_scr = self._hq0
                kn_sm = en1(nc.sbuf_tensor("kn_sm", [128, 4], F32))
                qrow = en1(nc.sbuf_tensor("qrow", [1, NH], F32))
                qbf = en1(nc.sbuf_tensor("qbf", [NH, 1], BF16))
                kn_bf = en1(nc.sbuf_tensor("kn_bf", [128, 2], BF16))
                for k in range(DC):
                    self.do("gpsimd", lambda E, k=k: E.dma_start(
                        wkv_sb[:, 128 * k:128 * k + 64],
                        wk[128 * k:128 * (k + 1), :]), inc=dw, inck=16)
                    self.do("gpsimd", lambda E, k=k: E.dma_start(
                        wkv_sb[:, 128 * k + 64:128 * (k + 1)],
                        wv[128 * k:128 * (k + 1), :]), inc=dw, inck=16)
                wt0 = dw.n
                t_wq = {0: 0, 1: 0}
                dwq = [self.sem(f"d_wq{i}") for i in range(2)]
                pa4 = 0
                ptr = {}
                alg_prev = 0
                prev_seq_v = 0
                for sqi in range(NSEQ):
                    tbase = sqi * QT
                    fw = [(dxs[sqi], dxs[sqi].n)]
                    if alg_prev:
                        fw.append((V, alg_prev))
                    lv, la = self.ln_stats(list(range(tbase, tbase + QT)),
                                           x_sb, ln_tmp, w_first=fw)
                    alg = self.ln_alg(QT, lv, la)
                    alg_prev = alg
                    ln_done = 0
                    for tl in range(QT):
                        wtr_ = list([(A, pa4)] if pa4 else ())
                        if sqi == 1 and tl == 0 and prev_seq_v:
                            wtr_.append((V, prev_seq_v))
                        vn, a4, ttk = self.ln_norm_tr(
                            tbase + tl, tl, x_sb, ln_tmp, lnps[:], lnT, S,
                            128 * tl, alg, prev_tt=ptr.get(tl - 2, 0),
                            w_tr=wtr_)
                        ptr[tl] = ttk
                        pa4 = a4
                        ln_done = a4
                    ptr = {}
                    ev = {}
                    for mj in range(DC):
                        ib = mj % 2
                        for k in range(DC):
                            self.do("gpsimd",
                                    lambda E, k=k, mj=mj, ib=ib:
                                    E.dma_start(
                                wqr[ib][:, 128 * k:128 * (k + 1)],
                                wq[128 * k:128 * (k + 1),
                                   128 * mj:128 * (mj + 1)]),
                                waits=([(T, t_wq[ib])] if k == 0 and
                                       t_wq[ib] else []),
                                inc=dwq[ib], inck=16)
                        wt = (dwq[ib], dwq[ib].n)
                        for nq in range(SQ):
                            idx = mj * SQ + nq
                            pb = idx % 2
                            last = None
                            for k in range(DC):
                                wl = []
                                if nq == 0 and k == 0:
                                    wl = [wt]
                                    if mj == 0:
                                        wl.append((A, ln_done))
                                        if prev_seq_v:
                                            wl.append((V, prev_seq_v))
                                if k == 0 and idx >= 2:
                                    wl = wl + [(A, ev[idx - 2])]
                                last = self.do(
                                    "tensor",
                                    lambda E, mj=mj, nq=nq, k=k, ib=ib,
                                    pb=pb: E.matmul(
                                        qps[pb][:],
                                        wqr[ib][:, 128 * k:128 * (k + 1)],
                                        lnT[:, S * k + SW * nq:
                                            S * k + SW * (nq + 1)],
                                        start=(k == 0), stop=(k == DC - 1)),
                                    waits=wl,
                                    inc=T if k == DC - 1 else None)
                            ev[idx] = self.ed(
                                "scalar", lambda E, mj=mj, nq=nq, pb=pb,
                                sqi=sqi: E.copy(
                                    qT[sqi][:, S * mj + SW * nq:
                                            S * mj + SW * (nq + 1)],
                                    qps[pb][:]), waits=[(T, last)])
                        t_wq[ib] = last
                    n0 = DC * SQ
                    evk = 0
                    for nq in range(SQ):
                        idx = n0 + nq
                        pb = idx % 2
                        last = None
                        for k in range(DC):
                            wl = [(dw, wt0)] if nq == 0 and k == 0 else []
                            if k == 0 and idx >= 2:
                                wl = wl + [(A, ev[idx - 2])]
                            last = self.do("tensor",
                                           lambda E, nq=nq, k=k, pb=pb:
                                           E.matmul(
                                qps[pb][0:64, :],
                                wkv_sb[:, 128 * k:128 * k + 64],
                                lnT[:, S * k + SW * nq:S * k + SW * (nq + 1)],
                                start=(k == 0), stop=(k == DC - 1)),
                                waits=wl, inc=T if k == DC - 1 else None)
                        ev[idx] = self.ed("scalar",
                                          lambda E, nq=nq, pb=pb, sqi=sqi:
                                          E.mul(
                            kT[sqi][0:64, SW * nq:SW * (nq + 1)],
                            qps[pb][0:64, :], 0.125), waits=[(T, last)])
                        evk = self.ed("vector", lambda E, nq=nq, sqi=sqi:
                                      E.tensor_copy(
                            kT[sqi][64:128, SW * nq:SW * (nq + 1)],
                            kT[sqi][0:64, SW * nq:SW * (nq + 1)]),
                            waits=[(A, ev[idx])])
                    ones_t = self.ed("vector", lambda E, sqi=sqi: E.memset(
                        v_sb[sqi].rearrange("p (c u) -> p c u",
                                            u=65)[:, :, 64:65], 1.0))
                    evv = {}
                    n1 = n0 + SQ
                    for tb in range(QT):
                        idx = n1 + tb
                        pb = idx % 2
                        last = None
                        for k in range(DC):
                            wl = []
                            if k == 0:
                                if (idx - 2) in evv:
                                    wl = [(V, evv[idx - 2])]
                                elif (idx - 2) in ev:
                                    wl = [(A, ev[idx - 2])]
                            last = self.do("tensor",
                                           lambda E, tb=tb, k=k, pb=pb:
                                           E.matmul(
                                qps[pb][:, 0:64],
                                lnT[:, S * k + 128 * tb:S * k + 128 * (tb + 1)],
                                wkv_sb[:, 128 * k + 64:128 * (k + 1)],
                                start=(k == 0), stop=(k == DC - 1)),
                                waits=wl, inc=T if k == DC - 1 else None)
                        evv[idx] = self.ed("vector",
                                           lambda E, tb=tb, pb=pb, sqi=sqi:
                                           E.tensor_copy(
                            v_sb[sqi][:, 65 * tb:65 * tb + 64],
                            qps[pb][:, 0:64]), waits=[(T, last)])
                    vmx = self.ed("vector", lambda E: E.memset(
                        kn_sm[:, 0:1], 0.0),
                        waits=[(V, evv[n1 + QT - 1])]
                              + ([(A, self._kn_a2_prev)]
                                 if getattr(self, "_kn_a2_prev", 0)
                                 else []))
                    tkn = 0
                    tkprev = 0
                    for tb in range(QT):
                        lastk = None
                        for k in range(DC):
                            lastk = self.do("tensor",
                                            lambda E, tb=tb, k=k: E.matmul(
                                kps[:], 
                                lnT[:, S * k + 128 * tb:S * k + 128 * (tb + 1)],
                                wkv_sb[:, 128 * k:128 * k + 64],
                                start=(k == 0), stop=(k == DC - 1)),
                                waits=([(V, tkn)] if (k == 0 and tkn)
                                       else []),
                                inc=T if k == DC - 1 else None)
                        t1 = self.ed("scalar", lambda E: E.activation(
                            qn_scr[:, 0:64], kps[:], AF.Square,
                            accum_out=kn_sm[:, 1:2]),
                            waits=[(T, lastk)]
                                  + ([(A, tkprev)] if tkprev else []))
                        tkprev = t1
                        tkn = self.ed("vector", lambda E: E.tensor_max(
                            kn_sm[:, 0:1], kn_sm[:, 0:1], kn_sm[:, 1:2]),
                            waits=[(A, t1)]
                                  + ([(V, tkn)] if tkn else [(V, vmx)]))
                    vbf = self.ed("vector", lambda E: E.tensor_copy(
                        kn_bf[:, 0:1], kn_sm[:, 0:1]), waits=[(V, tkn)])
                    tt1 = self.do("tensor", lambda E: E.transpose(
                        lnps[0:1, 0:128], kn_bf[:, 0:1], self.ident[:]),
                        waits=[(V, vbf)], inc=self.es["tensor"])
                    v2_ = self.ed("vector", lambda E: E.reduce_max(
                        kn_sm[0:1, 1:2], lnps[0:1, 0:128],
                        axis=AX), waits=[(T, tt1)])
                    a2_ = self.ed("scalar", lambda E: E.activation(
                        kn_sm[0:1, 1:2], kn_sm[0:1, 1:2], AF.Sqrt),
                        waits=[(V, v2_)])
                    tb2 = self.ed("tensor", lambda E: E.matmul(
                        kps[:, 0:1], self.ones_row[:], kn_sm[0:1, 1:2],
                        start=True, stop=True), waits=[(A, a2_)])
                    vkf = self.ed("vector", lambda E, sqi=sqi:
                                  E.tensor_scalar_mul(
                        kn_sm[:, 2 + sqi:3 + sqi], kps[:, 0:1], -0.125),
                        waits=[(T, tb2)])
                    self._kn_a2_prev = a2_
                    t_nq = 0
                    pv_sq = 0
                    for cdim in range(DC):
                        aq = self.ed("scalar", lambda E, cdim=cdim, sqi=sqi:
                                     E.activation(
                            sq2[:], qT[sqi][:, S * cdim:S * (cdim + 1)],
                            AF.Square),
                            waits=[(A, ev[(cdim + 1) * SQ - 1])]
                                  + ([(A, pv_sq)] if pv_sq else [])
                                  + ([(T, t_nq)] if t_nq else []))
                        pv_sq = aq
                        for nq in range(SQ):
                            t_nq = self.do(
                                "tensor",
                                lambda E, cdim=cdim, nq=nq: E.matmul(
                                    nqps[nq][:],
                                    self.ones16[:, NH * cdim:
                                                NH * (cdim + 1)],
                                    sq2[:, SW * nq:SW * (nq + 1)],
                                    start=(cdim == 0), stop=(cdim == DC - 1)),
                                waits=([(A, aq)] if nq == 0 else [])
                                      + ([(V, vkf)] if (cdim == 0 and
                                                        nq == 0) else []),
                                inc=T)
                    ev_nq = 0
                    for nq in range(SQ):
                        ev_nq = self.ed("vector", lambda E, nq=nq:
                                        E.tensor_copy(
                            qn2[:, SW * nq:SW * (nq + 1)], nqps[nq][:]),
                            waits=[(T, t_nq)] if nq == 0 else ())
                    vqm = self.ed("vector", lambda E: E.reduce_max(
                        kn_sm[0:NH, 1:2], qn2[:, 0:S], axis=AX),
                        waits=[(V, ev_nq)])
                    aqm = self.ed("scalar", lambda E: E.activation(
                        kn_sm[0:NH, 1:2], kn_sm[0:NH, 1:2], AF.Sqrt),
                        waits=[(V, vqm)])
                    vcb = self.ed("vector", lambda E: E.tensor_copy(
                        qbf[0:NH, 0:1], kn_sm[0:NH, 1:2]),
                        waits=[(A, aqm)])
                    tqr = self.do("tensor", lambda E: E.transpose(
                        lnps[0:1, 0:NH], qbf[0:NH, 0:1],
                        self.ident[0:NH, 0:NH]),
                        waits=[(V, vcb), (A, aqm)], inc=T)
                    vqr = self.ed("vector", lambda E: E.tensor_copy(
                        qrow[0:1, 0:NH], lnps[0:1, 0:NH]),
                        waits=[(T, tqr)])
                    tqb = self.do("tensor", lambda E: E.matmul(
                        qps[0][:, 0:NH], self.ones_row[:],
                        qrow[0:1, 0:NH], start=True, stop=True),
                        waits=[(V, vqr)], inc=T)
                    fin_tick = self.ed("vector", lambda E, sqi=sqi:
                                       E.tensor_scalar_mul(
                        biasK[:, NH * sqi:NH * (sqi + 1)],
                        qps[0][:, 0:NH], kn_sm[:, 2 + sqi:3 + sqi]),
                        waits=[(T, tqb), (V, vkf)])
                    proj_fin[sqi] = [(A, ev[n0 + SQ - 1]),
                                     (V, evv[n1 + QT - 1]), (A, aqm),
                                     (V, evk), (V, ones_t), (V, fin_tick)]
                    prev_seq_v = fin_tick
            with contextlib.ExitStack() as st2:
                en2 = st2.enter_context
                NTP = max(1, QT // 8)
                PHW = 512 * NTP
                QT4 = max(1, QT // 4)
                probsT = en2(nc.sbuf_tensor("probsTA", [128, 4 * PHW],
                                            BF16))
                sc = en2(nc.psum_tensor("scA", [128, 2 * PHW], F32))
                attps = [en2(nc.psum_tensor(f"attpsA{i}", [128, 65], F32))
                         for i in range(4)]

                def emit_scores(g, ph, ct, w_):
                    sqi = g % 2
                    i = g // 2
                    h, qt4 = divmod(i, QT4)
                    hj, hp = divmod(h, 2)
                    kc = NTP * ph + ct
                    return self.do(
                        "tensor", lambda E, sqi=sqi, hj=hj, hp=hp,
                        qt4=qt4, ct=ct, kc=kc, ph=ph: E.matmul(
                            sc[:, PHW * (ph % 2) + 512 * ct:
                               PHW * (ph % 2) + 512 * (ct + 1)],
                            kT[sqi][64 * hp:64 * hp + 64,
                                    128 * kc:128 * (kc + 1)],
                            qT[sqi][64 * hp:64 * hp + 64,
                                    S * hj + 512 * qt4:
                                    S * hj + 512 * (qt4 + 1)],
                            start=True, stop=True),
                        waits=w_, inc=T)

                NPH = QT // NTP

                def emit_att_ph(g, ph, w, sttw):
                    sqi = g % 2
                    last = None
                    nm = 0
                    for qh in range(4):
                        for ct in range(NTP):
                            kc = NTP * ph + ct
                            nm += 1
                            ww_ = list(w) if nm == 1 else []
                            if ct == 0 and sttw is not None:
                                ww_ = ww_ + sttw[qh]
                            last = self.do(
                                "tensor",
                                lambda E, sqi=sqi, ph=ph, ct=ct, kc=kc,
                                qh=qh: E.matmul(
                                    attps[qh][:],
                                    probsT[:, PHW * (ph % 4) + 512 * ct +
                                           128 * qh:
                                           PHW * (ph % 4) + 512 * ct +
                                           128 * (qh + 1)],
                                    v_sb[sqi][:, 65 * kc:65 * (kc + 1)],
                                    start=(ph == 0 and ct == 0),
                                    stop=(ph == NPH - 1 and
                                          ct == NTP - 1)),
                                waits=ww_,
                                inc=T if nm == 4 * NTP else None)
                    return last

                def x_slices(g, qh):
                    sqi = g % 2
                    i = g // 2
                    jh, jqt4 = divmod(i, QT4)
                    jqt = 4 * jqt4 + qh
                    xsl = x_sb[:, D * (sqi * QT + jqt) + 64 * jh:
                               D * (sqi * QT + jqt) + 64 * (jh + 1)]
                    return [(xsl, attps[qh][:, 0:64])]

                first = {0: proj_fin[0] + proj_fin[1], 1: proj_fin[1]}

                def bias_src(g):
                    sqi = g % 2
                    h = (g // 2) // QT4
                    return biasK[:, NH * sqi + h:NH * sqi + h + 1]

                self.attn_iters_t3(2 * NH * QT4, emit_scores, emit_att_ph,
                                   x_slices, probsT, sc,
                                   rec_src=lambda g, qh:
                                   attps[qh][:, 64:65],
                                   bias_src=bias_src, first_waits=first)
                if post is not None:
                    post()
            if end_hook is not None:
                end_hook()
            self.barrier()
            self.flush()

    def attn_b(self, x_sb, bq, bk, vo, end_hook=None):
        c, nc = self.cfg, self.nc
        D, S, DC, QT = c["D"], c["S"], c["DC"], c["QT"]
        SQ, SW, OW, DH, NSEQ = c["SQ"], c["SW"], c["OW"], c["DH"], c["NSEQ"]
        V, A, T = self.es["vector"], self.es["scalar"], self.es["tensor"]
        assert NSEQ == 2
        dw = self.sem("dw")
        with contextlib.ExitStack() as st:
            en = st.enter_context
            xq = [en(nc.sbuf_tensor(f"xqB{q}", [128, S], BF16))
                  for q in range(2)]
            xk = [en(nc.sbuf_tensor(f"xkB{q}", [128, S], BF16))
                  for q in range(2)]
            xvo = [en(nc.sbuf_tensor(f"xvoB{q}", [128, QT * D], BF16))
                   for q in range(2)]
            ln_tmp = [en(nc.sbuf_tensor("lntmpB0", [128, D], BF16)),
                      self._hq0]
            proj_fin = {}
            with contextlib.ExitStack() as st1:
                en1 = st1.enter_context
                lnT = en1(nc.sbuf_tensor("lnT_B", [128, DC * S], BF16))
                vobh = [en1(nc.sbuf_tensor(f"vobB{i}", [128, DC * OW],
                                           BF16)) for i in range(2)]
                wqk = en1(nc.sbuf_tensor("wqkB", [128, DC * 128], BF16))
                t_vob = {}
                lnps = en1(nc.psum_tensor("lnpsB", [128, DC * 128], BF16))
                qps = [en1(nc.psum_tensor(f"qpsB{i}", [128, SW], F32))
                       for i in range(2)]
                ppw = []
                for nm in ("d_ppo0", "d_ppo1"):
                    if nm in self._sems:
                        sp = self._sems[nm]
                        ppw.append((sp, sp.n))
                if ppw:
                    self.do("gpsimd", lambda E: E.nop(), waits=ppw)
                for k in range(DC):
                    self.do("gpsimd", lambda E, k=k: E.dma_start(
                        wqk[:, 128 * k:128 * k + 64],
                        bq[128 * k:128 * (k + 1), :]), inc=dw, inck=16)
                    self.do("gpsimd", lambda E, k=k: E.dma_start(
                        wqk[:, 128 * k + 64:128 * (k + 1)],
                        bk[128 * k:128 * (k + 1), :]), inc=dw, inck=16)
                wt = dw.n
                pa4 = 0
                ptr = {}
                alg_prev = 0
                prev_seq_v = 0
                for sqi in range(NSEQ):
                    tbase = sqi * QT
                    sb_ = 0
                    if sqi == 0 and self._pre is not None:
                        alg = self._pre
                        self._pre = None
                        sb_ = 64
                    else:
                        fw = [(V, alg_prev)] if alg_prev else []
                        lv, la = self.ln_stats(
                            list(range(tbase, tbase + QT)), x_sb, ln_tmp,
                            w_first=fw)
                        alg = self.ln_alg(QT, lv, la)
                    alg_prev = alg
                    ln_done = 0
                    for tl in range(QT):
                        vn, a4, ttk = self.ln_norm_tr(
                            tbase + tl, tl, x_sb, ln_tmp, lnps[:], lnT, S,
                            128 * tl, alg, prev_tt=ptr.get(tl - 2, 0),
                            w_tr=([(A, pa4)] if pa4 else ()), sb=sb_)
                        ptr[tl] = ttk
                        pa4 = a4
                        ln_done = a4
                    ptr = {}
                    ev = {}
                    evd = {}
                    for which in range(2):
                        dst = xq[sqi] if which == 0 else xk[sqi]
                        col0 = 0 if which == 0 else 64
                        scl = 1.0 if which == 0 else 1.0 / float(D) ** 0.5
                        for nq in range(SQ):
                            idx = which * SQ + nq
                            pb = idx % 2
                            last = None
                            for k in range(DC):
                                wl = []
                                if idx == 0 and k == 0:
                                    wl = [(dw, wt), (A, ln_done)]
                                    if prev_seq_v:
                                        wl.append((V, prev_seq_v))
                                if k == 0 and idx >= 2:
                                    wl = wl + [(A, ev[idx - 2])]
                                last = self.do(
                                    "tensor",
                                    lambda E, nq=nq, k=k, pb=pb, col0=col0:
                                    E.matmul(
                                        qps[pb][0:64, :],
                                        wqk[:, 128 * k + col0:
                                            128 * k + col0 + 64],
                                        lnT[:, S * k + SW * nq:
                                            S * k + SW * (nq + 1)],
                                        start=(k == 0), stop=(k == DC - 1)),
                                    waits=wl,
                                    inc=T if k == DC - 1 else None)
                            ev[idx] = self.ed(
                                "scalar", lambda E, nq=nq, pb=pb, dst=dst,
                                scl=scl: E.mul(
                                    dst[0:64, SW * nq:SW * (nq + 1)],
                                    qps[pb][0:64, :], scl),
                                waits=[(T, last)])
                            evd[idx] = self.ed(
                                "vector", lambda E, nq=nq, dst=dst:
                                E.tensor_copy(
                                    dst[64:128, SW * nq:SW * (nq + 1)],
                                    dst[0:64, SW * nq:SW * (nq + 1)]),
                                waits=[(A, ev[idx])])
                    evv = {}
                    n0 = 2 * SQ
                    for dh in range(DH):
                        vp = (sqi * DH + dh) % 2
                        dsem = self.sem(f"d_vo{vp}")
                        for k in range(DC):
                            self.do("gpsimd", lambda E, k=k, dh=dh, vp=vp:
                                    E.dma_start(
                                vobh[vp][:, OW * k:OW * (k + 1)],
                                vo[128 * k:128 * (k + 1),
                                   OW * dh:OW * (dh + 1)]),
                                waits=([(T, t_vob.get(vp, 0))]
                                       if (k == 0 and t_vob.get(vp, 0))
                                       else []), inc=dsem, inck=16)
                        wtv = dsem.n
                        for tb in range(QT):
                            idx = n0 + dh * QT + tb
                            pb = idx % 2
                            last = None
                            for k in range(DC):
                                wl = []
                                if k == 0:
                                    if tb == 0:
                                        wl = [(dsem, wtv)]
                                    if (idx - 2) in evv:
                                        wl = wl + [(V, evv[idx - 2])]
                                    elif (idx - 2) in ev:
                                        wl = wl + [(A, ev[idx - 2])]
                                last = self.do(
                                    "tensor",
                                    lambda E, tb=tb, dh=dh, k=k, pb=pb,
                                    vp=vp: E.matmul(
                                        qps[pb][:, 0:OW],
                                        lnT[:, S * k + 128 * tb:
                                            S * k + 128 * (tb + 1)],
                                        vobh[vp][:, OW * k:OW * (k + 1)],
                                        start=(k == 0), stop=(k == DC - 1)),
                                    waits=wl,
                                    inc=T if k == DC - 1 else None)
                            evv[idx] = self.ed(
                                "vector", lambda E, tb=tb, dh=dh, pb=pb,
                                sqi=sqi: E.tensor_copy(
                                    xvo[sqi][:, D * tb + OW * dh:
                                             D * tb + OW * (dh + 1)],
                                    qps[pb][:, 0:OW]), waits=[(T, last)])
                        t_vob[vp] = last
                    proj_fin[sqi] = [(A, ev[2 * SQ - 1]),
                                     (V, evv[n0 + QT * DH - 1]),
                                     (V, evd[2 * SQ - 1])]
                    prev_seq_v = evv[n0 + QT * DH - 1]
            with contextlib.ExitStack() as st2:
                en2 = st2.enter_context
                probs = en2(nc.sbuf_tensor("probsB", [128, 2 * S], BF16))
                probsT = en2(nc.sbuf_tensor("probsTB", [128, 2 * S], BF16))
                sc = en2(nc.psum_tensor("scB", [128, S], F32))
                psT = en2(nc.psum_tensor("psTB", [128, S], BF16))
                attps = en2(nc.psum_tensor("attpsB", [128, D], F32))

                def emit_scores(g, cq, w_):
                    sqi = g % 2
                    i = g // 2
                    qt = i
                    hp = i % 2
                    return self.do(
                        "tensor", lambda E, sqi=sqi, qt=qt, hp=hp,
                        cq=cq: E.matmul(
                            sc[:, SW * cq:SW * (cq + 1)],
                            xq[sqi][64 * hp:64 * hp + 64,
                                    128 * qt:128 * (qt + 1)],
                            xk[sqi][64 * hp:64 * hp + 64,
                                    SW * cq:SW * (cq + 1)],
                            start=True, stop=True),
                        waits=w_, inc=T)

                def emit_att(g, w, wmid=()):
                    sqi = g % 2
                    gb = g % 2
                    w = list(w)
                    if g >= 1:
                        w.append((V, self._t_stt_cur.get(g - 1, 0)))
                    last = None
                    nmm = 0
                    for cc in range(QT):
                        for dh in range(DH):
                            nmm += 1
                            ww_ = (w if nmm == 1 else
                                   wmid if (cc == QT // 2 and dh == 0)
                                   else ())
                            last = self.do(
                                "tensor", lambda E, sqi=sqi, gb=gb, cc=cc,
                                dh=dh: E.matmul(
                                    attps[:, OW * dh:OW * (dh + 1)],
                                    probsT[:, S * gb + 128 * cc:
                                           S * gb + 128 * (cc + 1)],
                                    xvo[sqi][:, D * cc + OW * dh:
                                             D * cc + OW * (dh + 1)],
                                    start=(cc == 0), stop=(cc == QT - 1)),
                                waits=ww_,
                                inc=T if nmm == QT * DH else None)
                    return last

                def x_slices(g):
                    sqi = g % 2
                    qt = g // 2
                    out = []
                    for dh in range(DH):
                        xsl = x_sb[:, D * (sqi * QT + qt) + OW * dh:
                                   D * (sqi * QT + qt) + OW * (dh + 1)]
                        out.append((xsl, attps[:, OW * dh:OW * (dh + 1)]))
                    return out

                first = {0: proj_fin[0] + proj_fin[1], 1: proj_fin[1]}
                self.attn_iters(2 * QT, emit_scores, emit_att, x_slices,
                                probs, probsT, sc, psT, None, 1.0, first,
                                split_evac=(S >= 2048))
            self.barrier()
            self.flush()

    def ffn(self, x_sb, w0s, w1s, w2s, yout, hooks=None):
        c, nc = self.cfg, self.nc
        D, DC, FC, TT, NTT, TG = (c["D"], c["DC"], c["FC"], c["TT"],
                                  c["NTT"], c["TG"])
        OW, DH = c["OW"], c["DH"]
        V, A, T = self.es["vector"], self.es["scalar"], self.es["tensor"]
        MG = min(4, FC)
        NMG = FC // MG
        KQ = min(8, FC)
        NKQ = FC // KQ
        final = yout is not None
        tag = "g" if final else "f"
        with contextlib.ExitStack() as st:
            en = st.enter_context
            ln_tmp = [en(nc.sbuf_tensor(f"{tag}_lntmp{i}", [128, D], BF16))
                      for i in range(2)]
            sq_tmp = [en(nc.sbuf_tensor(f"{tag}_lnsq{i}", [128, D], BF16))
                      for i in range(2)]
            lnT = en(nc.sbuf_tensor(f"{tag}_lnT", [128, DC * TT], BF16))
            h1 = en(nc.sbuf_tensor(f"{tag}_h1", [128, FC * TT], BF16))
            h2 = en(nc.sbuf_tensor(f"{tag}_h2", [128, FC * TT], BF16))
            w0b = [en(nc.sbuf_tensor(f"{tag}_w0{i}", [128, DC * 512], BF16))
                   for i in range(2)]
            w1b = [en(nc.sbuf_tensor(f"{tag}_w1{i}",
                                     [128, KQ * 128 * min(2, FC)], BF16))
                   for i in range(4)]
            w2b = [en(nc.sbuf_tensor(f"{tag}_w2{i}", [128, D], BF16))
                   for i in range(4)]
            stg = ([en(nc.sbuf_tensor(f"{tag}_st{i}", [128, OW], F32))
                    for i in range(4)] if final else None)
            h1ps = [en(nc.psum_tensor(f"{tag}_h1ps{i}", [128, TT], F32))
                    for i in range(2)]
            h2ps = [en(nc.psum_tensor(f"{tag}_h2ps{i}", [128, TT], F32))
                    for i in range(MG)]
            outps = en(nc.psum_tensor(f"{tag}_outps", [128, 1024], F32))
            d0 = [self.sem(f"d_{tag}0{i}") for i in range(2)]
            d1 = [self.sem(f"d_{tag}1{i}") for i in range(4)]
            d2 = [self.sem(f"d_{tag}2{i}") for i in range(4)]
            dy = ([self.sem(f"d_y{i}") for i in range(4)]
                  if final else None)
            assert DC * 128 * 2 <= TT * 4

            t_w0 = {0: 0, 1: 0}
            t_w1 = {0: 0, 1: 0, 2: 0, 3: 0}
            t_w2 = {0: 0, 1: 0, 2: 0, 3: 0}
            t_stg = [0, 0, 0, 0]
            t_h2evac = 0
            t_add = 0
            lnview = h2ps[0][:].bitcast(BF16)
            state = {"ptr": {}, "pa4": 0, "t_h2evac": 0}

            def emit_stats(tt):
                tiles = [(TT * tt) // 128 + tg for tg in range(TG)]
                lv, la = self.ln_stats(tiles, x_sb, sq_tmp)
                return self.ln_alg(TG, lv, la)

            def emit_norm(tt, alg, sb=0):
                ld = 0
                for tg in range(TG):
                    tglob = (TT * tt) // 128 + tg
                    wtr = [(A, state["pa4"])] if state["pa4"] else []
                    if tg == 0 and state["t_h2evac"]:
                        wtr.append((A, state["t_h2evac"]))
                    vn, a4, ttk = self.ln_norm_tr(
                        tglob, tg, x_sb, ln_tmp, lnview, lnT, TT, 128 * tg,
                        alg, prev_tt=state["ptr"].get(tg - 2, 0), w_tr=wtr,
                        sb=sb)
                    state["ptr"][tg] = ttk
                    state["pa4"] = a4
                    ld = a4
                return ld

            if self._pre is not None:
                next_ln = emit_norm(0, self._pre, sb=64)
                self._pre = None
            else:
                next_ln = emit_norm(0, emit_stats(0))
            alg_next = None
            for tt in range(NTT):
                ln_done = next_ln
                ev_h1 = {}
                for mg in range(NMG):
                    ib = mg % 2
                    self.do("sync", lambda E, mg=mg, ib=ib: E.dma_start(
                        w0b[ib][:].rearrange("p (k w) -> p k w", k=DC),
                        w0s[:, 512 * mg:512 * (mg + 1)]
                        .rearrange("(k p) w -> p k w", p=128)),
                        waits=([(T, t_w0[ib])] if t_w0[ib]
                               else list(getattr(self, "_pp_w0", []))),
                        inc=d0[ib], inck=16)
                    for m4 in range(MG):
                        m = MG * mg + m4
                        pb = m % 2
                        last = None
                        for k in range(DC):
                            wl = []
                            if m4 == 0 and k == 0:
                                wl = [(d0[ib], d0[ib].n), (A, ln_done)]
                                if mg == 0 and t_add:
                                    wl.append((V, t_add))
                            if k == 0 and m >= 2:
                                wl = wl + [(A, ev_h1[m - 2])]
                            last = self.do(
                                "tensor",
                                lambda E, m4=m4, k=k, ib=ib, pb=pb:
                                E.matmul(
                                    h1ps[pb][:],
                                    w0b[ib][:, 512 * k + 128 * m4:
                                            512 * k + 128 * (m4 + 1)],
                                    lnT[:, TT * k:TT * (k + 1)],
                                    start=(k == 0), stop=(k == DC - 1)),
                                waits=wl, inc=T if k == DC - 1 else None)
                        if m4 == MG - 1:
                            t_w0[ib] = last
                        ev_h1[m] = self.ed("scalar", lambda E, m=m, pb=pb:
                                           E.activation(
                            h1[:, TT * m:TT * (m + 1)], h1ps[pb][:],
                            AF.Relu), waits=[(T, last)])
                if tt + 1 < NTT:
                    alg_next = emit_stats(tt + 1)
                ev_h2 = {}
                M1G = min(2, FC)
                N1G = FC // M1G
                for mg in range(N1G):
                    pr = mg % 2
                    last_mm = None
                    for q in range(NKQ):
                        ib = (mg * NKQ + q) % 4
                        self.do("sync", lambda E, mg=mg, q=q, ib=ib,
                                M1G=M1G: E.dma_start(
                            w1b[ib][:, 0:KQ * 128 * M1G]
                            .rearrange("p (k w) -> p k w", k=KQ),
                            w1s[KQ * 128 * q:KQ * 128 * (q + 1),
                                128 * M1G * mg:128 * M1G * (mg + 1)]
                            .rearrange("(k p) w -> p k w", p=128)),
                            waits=([(T, t_w1[ib])] if t_w1[ib]
                                   else list(getattr(self, "_pp_w1", []))),
                            inc=d1[ib], inck=16)
                        for k8 in range(KQ):
                            for m4 in range(M1G):
                                wl = []
                                if k8 == 0 and m4 == 0:
                                    wl = [(d1[ib], d1[ib].n)]
                                    if mg <= 1:
                                        wl.append(
                                            (A, ev_h1[min(KQ * (q + 1),
                                                          FC) - 1]))
                                    elif q == 0:
                                        wl.append(
                                            (A, ev_h2[M1G * (mg - 1) - 1]))
                                last_mm = self.do(
                                    "tensor",
                                    lambda E, q=q, k8=k8, m4=m4, ib=ib,
                                    pr=pr, M1G=M1G: E.matmul(
                                        h2ps[2 * pr + m4][:],
                                        w1b[ib][:, 128 * M1G * k8 +
                                                128 * m4:
                                                128 * M1G * k8 +
                                                128 * (m4 + 1)],
                                        h1[:, TT * (KQ * q + k8):
                                           TT * (KQ * q + k8 + 1)],
                                        start=(q == 0 and k8 == 0),
                                        stop=(q == NKQ - 1 and
                                              k8 == KQ - 1)),
                                    waits=wl,
                                    inc=T if (k8 == KQ - 1 and
                                              m4 == M1G - 1) else None)
                        t_w1[ib] = last_mm
                    for m4 in range(M1G):
                        m = M1G * mg + m4
                        ev_h2[m] = self.ed("scalar",
                                           lambda E, m=m, m4=m4, pr=pr:
                                           E.activation(
                            h2[:, TT * m:TT * (m + 1)],
                            h2ps[2 * pr + m4][:],
                            AF.Relu), waits=[(T, last_mm)])
                t_h2evac = ev_h2[FC - 1]
                state["t_h2evac"] = t_h2evac
                if tt + 1 < NTT:
                    next_ln = emit_norm(tt + 1, alg_next)
                accs4 = ([outps[:, 0:OW], outps[:, 512:512 + OW]]
                         + [h1ps[i][:, 0:OW] for i in range(2)])
                PT = max(1, 4 // DH)
                for p0 in range(0, TG, PT):
                    tgs = list(range(p0, min(p0 + PT, TG)))
                    last_mm = None
                    for k in range(FC):
                        ib = k % 4
                        self.do("sync", lambda E, k=k, ib=ib: E.dma_start(
                            w2b[ib][:], w2s[128 * k:128 * (k + 1), :]),
                            waits=([(T, t_w2[ib])] if t_w2[ib]
                                   else list(getattr(self, "_pp_w2", []))),
                            inc=d2[ib], inck=16)
                        for ti, tg in enumerate(tgs):
                            for dh in range(DH):
                                wl = []
                                if ti == 0 and dh == 0:
                                    wl = [(d2[ib], d2[ib].n)]
                                    if k == 0:
                                        wl += [(A, t_h2evac), (V, t_add)]
                                acc = accs4[ti * DH + dh]
                                last_mm = self.do(
                                    "tensor",
                                    lambda E, k=k, tg=tg, dh=dh, ib=ib,
                                    acc=acc: E.matmul(
                                        acc,
                                        h2[:, TT * k + 128 * tg:
                                           TT * k + 128 * (tg + 1)],
                                        w2b[ib][:, OW * dh:OW * (dh + 1)],
                                        start=(k == 0), stop=(k == FC - 1)),
                                    waits=wl,
                                    inc=T if (ti == len(tgs) - 1 and
                                              dh == DH - 1) else None)
                        t_w2[ib] = last_mm
                    for ti, tg in enumerate(tgs):
                        tglob = (TT * tt) // 128 + tg
                        for dh in range(DH):
                            xsl = x_sb[:, D * tglob + OW * dh:
                                       D * tglob + OW * (dh + 1)]
                            src = accs4[ti * DH + dh]
                            if final:
                                sb = (ti * DH + dh) % 4
                                t_add = self.ed(
                                    "vector",
                                    lambda E, xsl=xsl, src=src, sb=sb:
                                    E.tensor_add(stg[sb][:], xsl, src),
                                    waits=[(T, last_mm)]
                                    + ([(dy[sb], t_stg[sb])] if t_stg[sb]
                                       else []))
                                t_stg[sb] = self.do(
                                    "scalar",
                                    lambda E, tglob=tglob, dh=dh, sb=sb:
                                    E.dma_start(
                                        yout[128 * tglob:
                                             128 * (tglob + 1),
                                             OW * dh:OW * (dh + 1)],
                                        stg[sb][:]),
                                    waits=[(V, t_add)], inc=dy[sb], inck=16)
                            else:
                                t_add = self.ed(
                                    "vector", lambda E, xsl=xsl, src=src:
                                    E.tensor_add(xsl, xsl, src),
                                    waits=[(T, last_mm)])
                if hooks and tt in hooks:
                    hooks[tt]()
            if final:
                self.do("scalar", lambda E: E.nop(),
                        waits=[(d, d.n) for d in dy])
            self.barrier()
            self.flush()


def build_mk(cfg):
    return MK(cfg).build()


def host_prep(inputs, n_cores=8):
    x = np.ascontiguousarray(np.asarray(inputs["L0_attn_a_x"]))
    B = x.shape[0]
    per = B // n_cores
    shared = dict(
        wq=np.ascontiguousarray(np.asarray(inputs["L0_attn_a_Wq"]).T),
        wk=np.ascontiguousarray(np.asarray(inputs["L0_attn_a_Wk"]).T),
        wv=np.ascontiguousarray(np.asarray(inputs["L0_attn_a_Wv"]).T),
        bq=np.ascontiguousarray(np.asarray(inputs["L0_attn_b_Q"]).T),
        bk=np.ascontiguousarray(np.asarray(inputs["L0_attn_b_K"]).T),
        vo=np.ascontiguousarray(np.asarray(inputs["L0_attn_b_VO"])),
        k0=np.ascontiguousarray(np.asarray(inputs["L0_ffn_k0"])).view(np.uint8),
        k1=np.ascontiguousarray(np.asarray(inputs["L0_ffn_k1"])).view(np.uint8),
        k2=np.ascontiguousarray(np.asarray(inputs["L0_ffn_k2"])).view(np.uint8),
        s0=np.asarray(inputs["L0_ffn_s0"], dtype=np.float32).reshape(1, 1),
        s1=np.asarray(inputs["L0_ffn_s1"], dtype=np.float32).reshape(1, 1),
        s2=np.asarray(inputs["L0_ffn_s2"], dtype=np.float32).reshape(1, 1),
    )
    in_maps = []
    for cidx in range(n_cores):
        m = dict(shared)
        m["xin"] = np.ascontiguousarray(
            x[cidx * per:(cidx + 1) * per].reshape(-1, x.shape[2]))
        in_maps.append(m)
    return in_maps


def kernel(**inputs):
    cfg = REAL_CFG
    n_cores = 8
    nc = build_mk(cfg)
    in_maps = host_prep(inputs, n_cores)
    res = run_bass_kernel_spmd(nc, in_maps, core_ids=list(range(n_cores)))
    B, S, D = np.asarray(inputs["L0_attn_a_x"]).shape
    per = B // n_cores
    out = np.concatenate(
        [np.asarray(r["y"]).reshape(per, S, D) for r in res.results], axis=0)
    return out.astype(np.float32)


# version 45
# speedup vs baseline: 1.0972x; 1.0972x over previous
import contextlib
import numpy as np

import concourse.bass as bass
import concourse.mybir as mybir
from concourse.bass_utils import run_bass_kernel_spmd

F32 = mybir.dt.float32
BF16 = mybir.dt.bfloat16
U8 = mybir.dt.uint8
AF = mybir.ActivationFunctionType
ALU = mybir.AluOpType
AX = mybir.AxisListType.X

ENGINES = ("tensor", "vector", "scalar", "gpsimd", "sync")

REAL_CFG = dict(NSEQ=2, S=2048, D=1024, NH=16, FF=4096, TT=512)


class Sem:
    __slots__ = ("h", "n", "name")

    def __init__(self, nc, name):
        self.h = nc.alloc_semaphore(name)
        self.n = 0
        self.name = name

    def tick(self, k=1):
        self.n += k
        return self.n


class MK:

    def __init__(self, cfg):
        self.cfg = dict(cfg)
        c = self.cfg
        c["DC"] = c["D"] // 128
        c["QT"] = c["S"] // 128
        c["NT"] = c["NSEQ"] * c["S"] // 128
        c["FC"] = c["FF"] // 128
        c["NTT"] = c["NSEQ"] * c["S"] // c["TT"]
        c["TG"] = c["TT"] // 128
        c["SW"] = min(512, c["S"])
        c["SQ"] = c["S"] // c["SW"]
        c["OW"] = min(512, c["D"])
        c["DH"] = c["D"] // c["OW"]
        assert 64 * c["NH"] == c["D"]
        self.nc = bass.Bass("TRN2", target_bir_lowering=False)
        self.q = {e: [] for e in ENGINES}
        self.es = {e: Sem(self.nc, f"es_{e}") for e in ENGINES}
        self.bar = Sem(self.nc, "bar")
        self._sems = {}
        self.ident = None

    def sem(self, name):
        if name not in self._sems:
            self._sems[name] = Sem(self.nc, name)
        return self._sems[name]

    def do(self, eng, fn, waits=(), inc=None, inck=1):
        for s, v in waits:
            if v:
                self.q[eng].append(lambda E, h=s.h, v=v: E.wait_ge(h, v))
        tick = None
        if inc is not None:
            tick = inc.tick(inck)
            self.q[eng].append(
                lambda E, fn=fn, h=inc.h, k=inck: fn(E).then_inc(h, k))
        else:
            self.q[eng].append(fn)
        return tick

    def ed(self, eng, fn, waits=()):
        return self.do(eng, fn, waits=waits, inc=self.es[eng])

    def flush(self):
        qs, self.q = self.q, {e: [] for e in ENGINES}
        with self.nc.Block() as b:
            for e in ENGINES:
                if qs[e]:
                    getattr(b, e)(lambda E, fns=qs[e]: [f(E) for f in fns])

    def barrier(self):
        for e in ENGINES:
            self.do(e, lambda E: E.nop(), inc=self.bar)
        v = self.bar.n
        for e in ENGINES:
            self.do(e, lambda E, h=self.bar.h, v=v: E.wait_ge(h, v))

    def build(self):
        c, nc = self.cfg, self.nc
        NSEQ, S, D, FF = c["NSEQ"], c["S"], c["D"], c["FF"]
        NT = c["NT"]

        xin = nc.dram_tensor("xin", [NSEQ * S, D], F32, kind="ExternalInput")
        wq = nc.dram_tensor("wq", [D, D], F32, kind="ExternalInput")
        wk = nc.dram_tensor("wk", [D, 64], F32, kind="ExternalInput")
        wv = nc.dram_tensor("wv", [D, 64], F32, kind="ExternalInput")
        bq = nc.dram_tensor("bq", [D, 64], F32, kind="ExternalInput")
        bk = nc.dram_tensor("bk", [D, 64], F32, kind="ExternalInput")
        vo = nc.dram_tensor("vo", [D, D], F32, kind="ExternalInput")
        k0 = nc.dram_tensor("k0", [2 * D, FF], U8, kind="ExternalInput")
        k1 = nc.dram_tensor("k1", [FF, FF], U8, kind="ExternalInput")
        k2 = nc.dram_tensor("k2", [FF, D], U8, kind="ExternalInput")
        s0 = nc.dram_tensor("s0", [1, 1], F32, kind="ExternalInput")
        s1 = nc.dram_tensor("s1", [1, 1], F32, kind="ExternalInput")
        s2 = nc.dram_tensor("s2", [1, 1], F32, kind="ExternalInput")
        y = nc.dram_tensor("y", [NSEQ * S, D], F32, kind="ExternalOutput")
        w0s = nc.dram_tensor("w0s", [D, FF], BF16, kind="Internal")
        w1s = nc.dram_tensor("w1s", [FF, FF], BF16, kind="Internal")
        w2s = nc.dram_tensor("w2s", [FF, D], BF16, kind="Internal")

        with contextlib.ExitStack() as top:
            en = top.enter_context
            x_sb = en(nc.sbuf_tensor("x_sb", [128, NT * D], BF16))
            self.ident = en(nc.sbuf_tensor("ident", [128, 128], BF16))
            ones_row = en(nc.sbuf_tensor("ones_row", [1, 128], F32))
            self.ones_row = ones_row
            NH = c["NH"]
            DC = c["DC"]
            self.ones16 = en(nc.sbuf_tensor("ones16", [128, DC * NH],
                                            BF16))
            s_sb = en(nc.sbuf_tensor("s_sb", [1, 3], F32))
            scb = en(nc.sbuf_tensor("scb", [128, 9], F32))
            self.lnst = en(nc.sbuf_tensor("lnst", [128, 128], F32))
            self.negmax = en(nc.sbuf_tensor("negmax", [128, 2], F32))
            self.negmaxq = en(nc.sbuf_tensor("negmaxq",
                                             [128, 2 * c["SQ"]], F32))
            self.biasb = en(nc.sbuf_tensor("biasb", [128, 2], F32))
            self.sump = en(nc.sbuf_tensor("sump", [128, 2 * c["SQ"]], F32))
            self.ssum = en(nc.sbuf_tensor("ssum", [128, 2], F32))
            self.recb = en(nc.sbuf_tensor("recb", [128, 2], F32))

            dxs = [self.sem(f"dx{i}") for i in range(NSEQ)]
            self.init_consts(ones_row)
            self.load_x(xin, x_sb, dxs)
            self.scalars_bcast(s0, s1, s2, s_sb, scb, ones_row)
            for d in dxs:
                self.do("gpsimd", lambda E, d=d: E.nop(), waits=[(d, d.n)])
            self.flush()

            self._pre = None
            hq0 = en(nc.sbuf_tensor("hq0", [128, D], BF16))
            self._hq0 = hq0
            hoist_sq = [hq0, hq0]
            TG = c["TG"]

            def hoist(tiles, n):
                lv, la = self.ln_stats(tiles, x_sb, hoist_sq,
                                       w_first=[(self.es["vector"],
                                                 self.es["vector"].n),
                                                (self.es["scalar"],
                                                 self.es["scalar"].n)],
                                       sb=64)
                self._pre = self.ln_alg(n, lv, la, sb=64)

            with contextlib.ExitStack() as pps:
                pen = pps.enter_context
                QWp = min(1024, FF)
                pp_tb = [pen(nc.sbuf_tensor(f"pp_t{i}", [128, QWp], BF16))
                         for i in range(3)]
                pp_bb = [pen(nc.sbuf_tensor(f"pp_b{i}", [128, QWp], BF16))
                         for i in range(2)]
                pp = lambda: self.prepass(k0, k1, k2, scb, w0s, w1s, w2s,
                                          (pp_tb, pp_bb))
                self.attn_a(x_sb, wq, wk, wv, dxs, post=pp,
                            end_hook=lambda: hoist(list(range(TG)), TG))
                self.ffn(x_sb, w0s, w1s, w2s, None, hooks={})
            self.attn_b(x_sb, bq, bk, vo,
                        end_hook=lambda: hoist(list(range(TG)), TG))
            self.ffn(x_sb, w0s, w1s, w2s, y, hooks={})
            self.flush()
        return nc

    def init_consts(self, ones_row):
        c = self.cfg
        P = self.es["gpsimd"]
        self.do("gpsimd", lambda E: E.memset(ones_row[:], 1.0), inc=P)
        t = self.do("gpsimd", lambda E: E.memset(self.ident[:], 1.0), inc=P)
        self.do("gpsimd", lambda E: E.affine_select(
            out=self.ident[:], in_=self.ident[:], compare_op=ALU.is_equal,
            fill=0.0, base=0, pattern=[[-1, 128]], channel_multiplier=1),
            waits=[(P, t)], inc=P)
        NH = c["NH"]
        tz = self.do("gpsimd", lambda E: E.memset(self.ones16[:], 0.0),
                     inc=P)
        pv = tz
        for cdim in range(c["DC"]):
            pv = self.do("gpsimd", lambda E, cdim=cdim, NH=NH: E.memset(
                self.ones16[0:64, NH * cdim + 2 * cdim:
                            NH * cdim + 2 * cdim + 1], 1.0),
                waits=[(P, pv)], inc=P)
            pv = self.do("gpsimd", lambda E, cdim=cdim, NH=NH: E.memset(
                self.ones16[64:128, NH * cdim + 2 * cdim + 1:
                            NH * cdim + 2 * cdim + 2], 1.0),
                waits=[(P, pv)], inc=P)
        self.ones16_tick = pv

    def load_x(self, xin, x_sb, dxs):
        c = self.cfg
        QT = c["QT"]
        for t in range(c["NT"]):
            self.do("gpsimd", lambda E, t=t: E.dma_start(
                x_sb[:, c["D"] * t:c["D"] * (t + 1)],
                xin[128 * t:128 * (t + 1), :]), inc=dxs[t // QT], inck=16)

    def scalars_bcast(self, s0, s1, s2, s_sb, scb, ones_row):
        nc = self.nc
        G, V, T = self.es["gpsimd"], self.es["vector"], self.es["tensor"]
        ds = self.sem("d_scal")
        self._scb_zero = self.do("gpsimd", lambda E: E.memset(
            scb[:, 8:9], 0.0), inc=self.es["gpsimd"])
        with nc.psum_tensor("pp_ps", [128, 1], F32) as psc:
            for j, sx in enumerate((s0, s1, s2)):
                self.do("gpsimd", lambda E, sx=sx, j=j: E.dma_start(
                    s_sb[:, j:j + 1], sx[:]), inc=ds, inck=16)
            tmul = [0, 0, 0]
            prev_v = 0
            for j in range(3):
                w = [(ds, ds.n), (G, G.n)] if j == 0 else [(V, prev_v)]
                t = self.ed("tensor", lambda E, j=j: E.matmul(
                    psc[:], ones_row[:], s_sb[:, j:j + 1],
                    start=True, stop=True), waits=w)
                prev_v = self.ed("vector", lambda E, j=j: E.tensor_copy(
                    scb[:, j:j + 1], psc[:]), waits=[(T, t)])
                tmul[j] = prev_v
            vt = 0
            for dst, srcc, mul in ((3, 0, 2.0), (4, 1, 4.0), (5, 1, -2.0),
                                   (6, 2, 4.0), (7, 2, -2.0)):
                vt = self.ed("vector", lambda E, d=dst, sr=srcc, m=mul:
                             E.tensor_scalar_mul(
                    scb[:, d:d + 1], scb[:, sr:sr + 1], m),
                    waits=[(V, tmul[srcc])])
            self.scal_tick = vt

    def prepass(self, k0, k1, k2, scb, w0s, w1s, w2s, bufs):
        c, nc = self.cfg, self.nc
        D, FF, DC, FC = c["D"], c["FF"], c["DC"], c["FC"]
        G = self.es["gpsimd"]
        V = self.es["vector"]
        QW = min(1024, FF)
        NQ = FF // QW
        tb, bb = bufs
        R = len(tb)
        din = [self.sem(f"d_pp{i}") for i in range(R)]
        dout = [self.sem(f"d_ppo{i}") for i in range(R)]
        free = {i: 0 for i in range(R)}
        stored = {i: 0 for i in range(R)}

        specs = []
        for ci in range(DC):
            for q in range(NQ):
                bsl = len(specs) % len(bb)
                specs.append((
                    (k0[128 * ci:128 * (ci + 1), QW * q:QW * (q + 1)],
                     k0[128 * (ci + DC):128 * (ci + DC + 1),
                        QW * q:QW * (q + 1)]),
                    w0s[128 * ci:128 * (ci + 1), QW * q:QW * (q + 1)],
                    [lambda E, ib, bsl=bsl: E.tensor_sub(
                        tb[ib][:], tb[ib][:], bb[bsl][:]),
                     lambda E, ib: E.tensor_scalar(
                         tb[ib][:], tb[ib][:], scb[:, 3:4], scb[:, 8:9],
                         op0=ALU.mult, op1=ALU.add)], QW))
        n_k0 = len(specs)
        for ci in range(FC):
            for q in range(NQ):
                specs.append((
                    (k1[128 * ci:128 * (ci + 1), QW * q:QW * (q + 1)],
                     None),
                    w1s[128 * ci:128 * (ci + 1), QW * q:QW * (q + 1)],
                    [lambda E, ib: E.tensor_scalar(
                        tb[ib][:], tb[ib][:], scb[:, 4:5], scb[:, 5:6],
                        op0=ALU.mult, op1=ALU.add)], QW))
        n_k1 = len(specs)
        NQ2 = max(1, D // QW)
        QW2 = min(QW, D)
        for ci in range(FC):
            for q in range(NQ2):
                specs.append((
                    (k2[128 * ci:128 * (ci + 1),
                        QW2 * q:QW2 * (q + 1)], None),
                    w2s[128 * ci:128 * (ci + 1), QW2 * q:QW2 * (q + 1)],
                    [lambda E, ib: E.tensor_scalar(
                        tb[ib][:, 0:QW2], tb[ib][:, 0:QW2],
                        scb[:, 6:7], scb[:, 7:8],
                        op0=ALU.mult, op1=ALU.add)], QW2))

        def emit_load(si):
            ib = si % R
            srcs, _, _, w = specs[si]
            for j, (t_, src) in enumerate(zip((tb, bb), srcs)):
                if src is None:
                    continue
                bi = si % len(t_)
                self.do("gpsimd", lambda E, t_=t_, src=src, bi=bi, w=w:
                        E.dma_start(t_[bi][:, 0:w], src),
                        waits=(([(G, free[ib])] if free[ib] else [])
                               + ([(dout[ib], stored[ib])]
                                  if stored[ib] else [])) if j == 0
                              else [], inc=din[ib], inck=16)

        N = len(specs)
        LEAD = min(2, N)
        for si in range(LEAD):
            emit_load(si)
        for si in range(N):
            _, dst, ops, w = specs[si]
            ib = si % R
            g = 0
            for fn in ops:
                g = self.do("gpsimd", lambda E, fn=fn, ib=ib: fn(E, ib),
                            waits=([(din[ib], din[ib].n),
                                    (V, self.scal_tick),
                                    (G, self._scb_zero)] if g == 0
                                   else [(G, g)]), inc=G)
            free[ib] = g
            self.do("gpsimd", lambda E, dst=dst, ib=ib, w=w:
                    E.dma_start(dst, tb[ib][:, 0:w]), waits=[(G, g)],
                    inc=dout[ib], inck=16)
            stored[ib] = dout[ib].n
            if si + LEAD < N:
                emit_load(si + LEAD)
            if si == n_k0 - 1:
                self._pp_w0 = [(d, d.n) for d in dout]
            if si == n_k1 - 1:
                self._pp_w1 = [(d, d.n) for d in dout]
        self._pp_w2 = [(d, d.n) for d in dout]

    def ln_stats(self, tiles, x_sb, sq_tmp, w_first=(), sb=0):
        c = self.cfg
        D = c["D"]
        lv = la = 0
        ring = 1 if sq_tmp[0] is sq_tmp[1] else 2
        lastsq = {0: 0, 1: 0}
        for j, t in enumerate(tiles):
            xt = x_sb[:, D * t:D * (t + 1)]
            lv = self.ed("vector", lambda E, xt=xt, j=j: E.reduce_sum(
                self.lnst[:, sb + j:sb + j + 1], xt, axis=AX),
                waits=list(w_first) if j == 0 else ())
            la = self.ed("scalar", lambda E, xt=xt, j=j: E.activation(
                sq_tmp[j % ring][:], xt, AF.Square,
                accum_out=self.lnst[:, sb + 16 + j:sb + 17 + j]),
                waits=(list(w_first) if j == 0 else [])
                      + ([(self.es["scalar"], lastsq[j % ring])]
                         if lastsq[j % ring] else []))
            lastsq[j % ring] = la
        return lv, la

    def ln_alg(self, n, lv, la, sb=0):
        c = self.cfg
        D = c["D"]
        V, A = self.es["vector"], self.es["scalar"]
        st = self.lnst
        m = st[:, sb:sb + n]
        q = st[:, sb + 16:sb + 16 + n]
        r = st[:, sb + 32:sb + 32 + n]
        nm = st[:, sb + 48:sb + 48 + n]
        v1 = self.ed("vector", lambda E: E.tensor_scalar_mul(m, m, 1.0 / D),
                     waits=[(V, lv)])
        v2 = self.ed("vector", lambda E: E.tensor_scalar(
            q, q, 1.0 / D, 1e-5, op0=ALU.mult, op1=ALU.add),
            waits=[(A, la), (V, v1)])
        v3 = self.ed("vector", lambda E: E.scalar_tensor_tensor(
            r, m, -1.0, m, op0=ALU.mult, op1=ALU.mult), waits=[(V, v2)])
        v4 = self.ed("vector", lambda E: E.tensor_add(q, q, r),
                     waits=[(V, v3)])
        a5 = self.ed("scalar", lambda E: E.activation(r, q, AF.Sqrt),
                     waits=[(V, v4)])
        v6 = self.ed("vector", lambda E: E.reciprocal(r, r),
                     waits=[(A, a5)])
        v7 = self.ed("vector", lambda E: E.scalar_tensor_tensor(
            nm, m, -1.0, r, op0=ALU.mult, op1=ALU.mult), waits=[(V, v6)])
        return v7

    def ln_norm_tr(self, t, j, x_sb, ln_tmp, lnps_ap, lnT, W, toff,
                   alg_tick, prev_tt=0, w_tr=(), w_a4=(), sb=0):
        c = self.cfg
        D, DC = c["D"], c["DC"]
        V, A, T = self.es["vector"], self.es["scalar"], self.es["tensor"]
        st = self.lnst
        xt = x_sb[:, D * t:D * (t + 1)]
        buf = ln_tmp[j % 2]
        vn = self.ed("vector", lambda E, xt=xt, j=j, buf=buf:
                     E.tensor_scalar(
            buf[:], xt, st[:, sb + 32 + j:sb + 33 + j],
            st[:, sb + 48 + j:sb + 49 + j],
            op0=ALU.mult, op1=ALU.add),
            waits=[(V, alg_tick)] + ([(T, prev_tt)] if prev_tt else []))
        tt = None
        for jj in range(DC):
            tt = self.do("tensor", lambda E, jj=jj, buf=buf: E.transpose(
                lnps_ap[:, 128 * jj:128 * (jj + 1)],
                buf[:, 128 * jj:128 * (jj + 1)], self.ident[:]),
                waits=([(V, vn)] + list(w_tr)) if jj == 0 else (),
                inc=T if jj == DC - 1 else None)
        dst = lnT.rearrange("p (j w) -> p j w", j=DC)[:, :, toff:toff + 128]
        src = lnps_ap[:, 0:DC * 128].rearrange("p (j w) -> p j w", j=DC)
        a4 = self.ed("scalar", lambda E, dst=dst, src=src: E.copy(dst, src),
                     waits=[(T, tt)] + list(w_a4))
        return vn, a4, tt

    def attn_iters(self, NIT, emit_scores, emit_att_info, x_slices,
                   probs, probsT, sc, psT, att_read_ap, scale,
                   first_waits, rec_src=None, split_evac=False,
                   bias_src=None, bias_tick=0):
        c = self.cfg
        S, QT, SQ, SW = c["S"], c["QT"], c["SQ"], c["SW"]
        V, A, T = self.es["vector"], self.es["scalar"], self.es["tensor"]
        t_sc, t_exp4, t_evac, t_tr, t_att, t_stt = {}, {}, {}, {}, {}, {}
        t_bias, t_recp, t_expq, t_trh, t_evA = {}, {}, {}, {}, {}
        self._t_stt_cur = t_stt

        def evw(tp):
            ea, ev = tp
            return ([(A, ea)] if ea else []) + ([(V, ev)] if ev else [])

        for i in range(NIT + 2):
            if i < NIT:
                if i < 1:
                    wq = [[] for _ in range(SQ)]
                else:
                    wq = [[(A, t_expq[i - 1][cq])] for cq in range(SQ)]
                wq[0] = wq[0] + list(first_waits.get(i, []))
                t_sc[i] = {}
                for cq in range(SQ):
                    t_sc[i][cq] = emit_scores(i, cq, wq[cq])
            if i >= 2:
                j = i - 2
                if split_evac:
                    w = [(V, t_evA[j])]
                    wmid = [(V, t_evac[j][1])]
                else:
                    w = evw(t_evac[j])
                    wmid = ()
                if j >= 2:
                    w = w + [(V, t_stt[j - 2])]
                t_att[j] = emit_att_info(j, w, wmid)
            if 1 <= i <= NIT:
                j = i - 1
                jb = j % 2
                if j >= 1:
                    w = evw(t_evac[j - 1]) + [(A, t_exp4[j])]
                else:
                    w = [(A, t_exp4[0])]
                HQ = QT // 2
                for cc in range(QT):
                    half_tick = split_evac and cc == HQ - 1
                    tick = self.do(
                        "tensor", lambda E, jb=jb, cc=cc:
                        E.transpose(
                            psT[:, 128 * cc:128 * (cc + 1)],
                            probs[:, S * jb + 128 * cc:
                                  S * jb + 128 * (cc + 1)],
                            self.ident[:]),
                        waits=w if cc == 0 else (),
                        inc=T if (cc == QT - 1 or half_tick) else None)
                    if half_tick:
                        t_trh[j] = tick
                    if cc == QT - 1:
                        t_tr[j] = tick
            if i < NIT and bias_src is None:
                ib = i % 2
                t_max = 0
                for cq in range(SQ):
                    wq_ = [(T, t_sc[i][cq])]
                    if cq == 0 and i >= 2:
                        wq_.append((V, t_bias[i - 2]))
                    t_max = self.ed("vector", lambda E, ib=ib, cq=cq:
                                    E.reduce_max(
                        self.negmaxq[:, SQ * ib + cq:SQ * ib + cq + 1],
                        sc[:, SW * cq:SW * (cq + 1)], axis=AX,
                        negate=True),
                        waits=wq_)
                t_bias[i] = self.ed("vector", lambda E, ib=ib:
                                    E.tensor_reduce(
                    self.biasb[:, ib:ib + 1],
                    self.negmaxq[:, SQ * ib:SQ * (ib + 1)], AX, ALU.min),
                    waits=[(V, t_max)])
            if i >= 2:
                j = i - 2
                jb = j % 2
                if rec_src is not None:
                    t_recp[j] = self.ed("vector",
                                        lambda E, jb=jb, a=rec_src(j):
                                        E.reciprocal(
                        self.recb[:, jb:jb + 1], a),
                        waits=[(T, t_att[j])]
                              + ([(V, t_stt[j - 2])] if j >= 2 else []))
                w = [(T, t_att[j]), (V, t_recp[j])]
                for xsl, asrc in x_slices(j):
                    t_stt[j] = self.ed("vector",
                                       lambda E, xsl=xsl, asrc=asrc, jb=jb:
                                       E.scalar_tensor_tensor(
                        xsl, asrc, self.recb[:, jb:jb + 1], xsl,
                        op0=ALU.mult, op1=ALU.add), waits=w)
                    w = []
            if i < NIT:
                ib = i % 2
                bias_ap = (self.biasb[:, ib:ib + 1] if bias_src is None
                           else bias_src(i))
                for cq in range(SQ):
                    w = []
                    if cq == 0:
                        if bias_src is None:
                            w = [(V, t_bias[i])]
                        elif i <= 1 and bias_tick:
                            w = [(V, bias_tick)]
                        if i >= 2:
                            w.append((T, t_tr[i - 2]))
                    if bias_src is not None:
                        w = w + [(T, t_sc[i][cq])]
                    acc = (None if rec_src is not None else
                           self.sump[:, SQ * ib + cq:SQ * ib + cq + 1])
                    t_exp4[i] = self.ed("scalar", lambda E, ib=ib, cq=cq,
                                        acc=acc, bias_ap=bias_ap:
                                        E.activation(
                        probs[:, S * ib + SW * cq:S * ib + SW * (cq + 1)],
                        sc[:, SW * cq:SW * (cq + 1)], AF.Exp,
                        bias=bias_ap, scale=1.0,
                        accum_out=acc),
                        waits=w)
                    t_expq.setdefault(i, {})[cq] = t_exp4[i]
            if 1 <= i <= NIT:
                j = i - 1
                jb = j % 2
                war = [(T, t_att[j - 2])] if j >= 2 else []
                if split_evac:
                    H = S // 2
                    t_evA[j] = self.ed(
                        "vector", lambda E, jb=jb, H=H:
                        E.tensor_copy(
                            probsT[:, S * jb:S * jb + H], psT[:, 0:H]),
                        waits=[(T, t_trh[j])] + war)
                    tv = self.ed("vector", lambda E, jb=jb, H=H:
                                 E.tensor_copy(
                        probsT[:, S * jb + H:S * (jb + 1)], psT[:, H:S]),
                        waits=[(T, t_tr[j])])
                    t_evac[j] = (0, tv)
                else:
                    t_evac[j] = (self.ed("scalar", lambda E, jb=jb: E.copy(
                        probsT[:, S * jb:S * (jb + 1)], psT[:]),
                        waits=[(T, t_tr[j])] + war), 0)
            if rec_src is None:
                if i < NIT:
                    ib = i % 2
                    t_sum = self.ed("vector", lambda E, ib=ib: E.reduce_sum(
                        self.ssum[:, ib:ib + 1],
                        self.sump[:, SQ * ib:SQ * (ib + 1)], axis=AX),
                        waits=[(A, t_exp4[i])]
                              + ([(V, t_recp[i - 2])] if i >= 2 else []))
                    t_recp[i] = self.ed("vector", lambda E, ib=ib:
                                        E.reciprocal(
                        self.recb[:, ib:ib + 1], self.ssum[:, ib:ib + 1]),
                        waits=[(V, t_sum)]
                              + ([(V, t_stt[i - 2])] if i >= 2 else []))
        return t_stt

    def attn_a(self, x_sb, wq, wk, wv, dxs, post=None, end_hook=None):
        c, nc = self.cfg, self.nc
        D, S, DC, QT, NH = c["D"], c["S"], c["DC"], c["QT"], c["NH"]
        SQ, SW, NSEQ = c["SQ"], c["SW"], c["NSEQ"]
        V, A, T = self.es["vector"], self.es["scalar"], self.es["tensor"]
        assert NSEQ == 2
        dw = self.sem("dw")
        with contextlib.ExitStack() as st:
            en = st.enter_context
            qT = [en(nc.sbuf_tensor(f"qT{q}", [128, DC * S], BF16))
                  for q in range(2)]
            kT = [en(nc.sbuf_tensor(f"kTd{q}", [128, S], BF16))
                  for q in range(2)]
            v_sb = [en(nc.sbuf_tensor(f"v_sb{q}", [128, QT * 65], BF16))
                    for q in range(2)]
            wkv_sb = en(nc.sbuf_tensor("wkv", [128, DC * 128], BF16))
            ln_tmp = [en(nc.sbuf_tensor(f"lntmpA{i}", [128, D], BF16))
                      for i in range(2)]
            NHQ = NH * QT
            biasC = [en(nc.sbuf_tensor(f"biasC{q}", [128, NHQ], F32))
                     for q in range(2)]
            proj_fin = {}
            fin_tick = 0
            with contextlib.ExitStack() as st1:
                en1 = st1.enter_context
                lnT = en1(nc.sbuf_tensor("lnT_A", [128, DC * S], BF16))
                wqr = [en1(nc.sbuf_tensor(f"wqr{i}", [128, DC * 128], BF16))
                       for i in range(2)]
                lnps = en1(nc.psum_tensor("lnpsA", [128, DC * 128], BF16))
                qps = [en1(nc.psum_tensor(f"qpsA{i}", [128, SW], F32))
                       for i in range(2)]
                kps = en1(nc.psum_tensor("kpsA", [128, 64], F32))
                sq2 = en1(nc.sbuf_tensor("sq2", [128, S], BF16))
                qn2 = en1(nc.sbuf_tensor("qn2", [NH, S], BF16))
                nqps = [en1(nc.psum_tensor(f"nqps{i}", [NH, SW], F32))
                        for i in range(SQ)]
                qn_scr = self._hq0
                kn_sm = en1(nc.sbuf_tensor("kn_sm", [128, 4], F32))
                kn_bf = en1(nc.sbuf_tensor("kn_bf", [128, 2], BF16))
                for k in range(DC):
                    self.do("gpsimd", lambda E, k=k: E.dma_start(
                        wkv_sb[:, 128 * k:128 * k + 64],
                        wk[128 * k:128 * (k + 1), :]), inc=dw, inck=16)
                    self.do("gpsimd", lambda E, k=k: E.dma_start(
                        wkv_sb[:, 128 * k + 64:128 * (k + 1)],
                        wv[128 * k:128 * (k + 1), :]), inc=dw, inck=16)
                wt0 = dw.n
                t_wq = {0: 0, 1: 0}
                dwq = [self.sem(f"d_wq{i}") for i in range(2)]
                pa4 = 0
                pev = {0: 0, 1: 0}
                ptr = {}
                alg_prev = 0
                prev_seq_v = 0
                for sqi in range(NSEQ):
                    tbase = sqi * QT
                    fw = [(dxs[sqi], dxs[sqi].n)]
                    if alg_prev:
                        fw.append((V, alg_prev))
                    lv, la = self.ln_stats(list(range(tbase, tbase + QT)),
                                           x_sb, ln_tmp, w_first=fw)
                    alg = self.ln_alg(QT, lv, la)
                    alg_prev = alg
                    ln_done = 0
                    lnpp = [lnps[:],
                            qps[0][:].bitcast(BF16)[:, 0:DC * 128]]
                    for tl in range(QT):
                        vn, a4, ttk = self.ln_norm_tr(
                            tbase + tl, tl, x_sb, ln_tmp,
                            lnpp[tl % 2], lnT, S,
                            128 * tl, alg, prev_tt=ptr.get(tl - 2, 0),
                            w_tr=([(A, pev[tl % 2])]
                                  if pev[tl % 2] else ()))
                        ptr[tl] = ttk
                        pev[tl % 2] = a4
                        ln_done = a4
                    ptr = {}
                    ev = {}
                    for mj in range(DC):
                        ib = mj % 2
                        for k in range(DC):
                            self.do("gpsimd",
                                    lambda E, k=k, mj=mj, ib=ib:
                                    E.dma_start(
                                wqr[ib][:, 128 * k:128 * (k + 1)],
                                wq[128 * k:128 * (k + 1),
                                   128 * mj:128 * (mj + 1)]),
                                waits=([(T, t_wq[ib])] if k == 0 and
                                       t_wq[ib] else []),
                                inc=dwq[ib], inck=16)
                        wt = (dwq[ib], dwq[ib].n)
                        for nq in range(SQ):
                            idx = mj * SQ + nq
                            pb = idx % 2
                            last = None
                            for k in range(DC):
                                wl = []
                                if nq == 0 and k == 0:
                                    wl = [wt]
                                    if mj == 0:
                                        wl.append((A, ln_done))
                                        if prev_seq_v:
                                            wl.append((V, prev_seq_v))
                                if k == 0 and idx >= 2:
                                    wl = wl + [(A, ev[idx - 2])]
                                last = self.do(
                                    "tensor",
                                    lambda E, mj=mj, nq=nq, k=k, ib=ib,
                                    pb=pb: E.matmul(
                                        qps[pb][:],
                                        wqr[ib][:, 128 * k:128 * (k + 1)],
                                        lnT[:, S * k + SW * nq:
                                            S * k + SW * (nq + 1)],
                                        start=(k == 0), stop=(k == DC - 1)),
                                    waits=wl,
                                    inc=T if k == DC - 1 else None)
                            ev[idx] = self.ed(
                                "scalar", lambda E, mj=mj, nq=nq, pb=pb,
                                sqi=sqi: E.copy(
                                    qT[sqi][:, S * mj + SW * nq:
                                            S * mj + SW * (nq + 1)],
                                    qps[pb][:]), waits=[(T, last)])
                        t_wq[ib] = last
                    n0 = DC * SQ
                    evk = 0
                    for nq in range(SQ):
                        idx = n0 + nq
                        pb = idx % 2
                        last = None
                        for k in range(DC):
                            wl = [(dw, wt0)] if nq == 0 and k == 0 else []
                            if k == 0 and idx >= 2:
                                wl = wl + [(A, ev[idx - 2])]
                            last = self.do("tensor",
                                           lambda E, nq=nq, k=k, pb=pb:
                                           E.matmul(
                                qps[pb][0:64, :],
                                wkv_sb[:, 128 * k:128 * k + 64],
                                lnT[:, S * k + SW * nq:S * k + SW * (nq + 1)],
                                start=(k == 0), stop=(k == DC - 1)),
                                waits=wl, inc=T if k == DC - 1 else None)
                        ev[idx] = self.ed("scalar",
                                          lambda E, nq=nq, pb=pb, sqi=sqi:
                                          E.mul(
                            kT[sqi][0:64, SW * nq:SW * (nq + 1)],
                            qps[pb][0:64, :], 0.125), waits=[(T, last)])
                        evk = self.ed("vector", lambda E, nq=nq, sqi=sqi:
                                      E.tensor_copy(
                            kT[sqi][64:128, SW * nq:SW * (nq + 1)],
                            kT[sqi][0:64, SW * nq:SW * (nq + 1)]),
                            waits=[(A, ev[idx])])
                    ones_t = self.ed("vector", lambda E, sqi=sqi: E.memset(
                        v_sb[sqi].rearrange("p (c u) -> p c u",
                                            u=65)[:, :, 64:65], 1.0))
                    evv = {}
                    n1 = n0 + SQ
                    for tb in range(QT):
                        idx = n1 + tb
                        pb = idx % 2
                        last = None
                        for k in range(DC):
                            wl = []
                            if k == 0:
                                if (idx - 2) in evv:
                                    wl = [(V, evv[idx - 2])]
                                elif (idx - 2) in ev:
                                    wl = [(A, ev[idx - 2])]
                            last = self.do("tensor",
                                           lambda E, tb=tb, k=k, pb=pb:
                                           E.matmul(
                                qps[pb][:, 0:64],
                                lnT[:, S * k + 128 * tb:S * k + 128 * (tb + 1)],
                                wkv_sb[:, 128 * k + 64:128 * (k + 1)],
                                start=(k == 0), stop=(k == DC - 1)),
                                waits=wl, inc=T if k == DC - 1 else None)
                        evv[idx] = self.ed("vector",
                                           lambda E, tb=tb, pb=pb, sqi=sqi:
                                           E.tensor_copy(
                            v_sb[sqi][:, 65 * tb:65 * tb + 64],
                            qps[pb][:, 0:64]), waits=[(T, last)])
                    vmx = self.ed("vector", lambda E: E.memset(
                        kn_sm[:, 0:1], 0.0), waits=[(V, evv[n1 + QT - 1])])
                    tkn = 0
                    tkprev = 0
                    for tb in range(QT):
                        lastk = None
                        for k in range(DC):
                            lastk = self.do("tensor",
                                            lambda E, tb=tb, k=k: E.matmul(
                                kps[:], 
                                lnT[:, S * k + 128 * tb:S * k + 128 * (tb + 1)],
                                wkv_sb[:, 128 * k:128 * k + 64],
                                start=(k == 0), stop=(k == DC - 1)),
                                waits=([(V, tkn)] if (k == 0 and tkn)
                                       else []),
                                inc=T if k == DC - 1 else None)
                        t1 = self.ed("scalar", lambda E: E.activation(
                            qn_scr[:, 0:64], kps[:], AF.Square,
                            accum_out=kn_sm[:, 1:2]),
                            waits=[(T, lastk)]
                                  + ([(A, tkprev)] if tkprev else []))
                        tkprev = t1
                        tkn = self.ed("vector", lambda E: E.tensor_max(
                            kn_sm[:, 0:1], kn_sm[:, 0:1], kn_sm[:, 1:2]),
                            waits=[(A, t1)]
                                  + ([(V, tkn)] if tkn else [(V, vmx)]))
                    vbf = self.ed("vector", lambda E: E.tensor_copy(
                        kn_bf[:, 0:1], kn_sm[:, 0:1]), waits=[(V, tkn)])
                    tt1 = self.do("tensor", lambda E: E.transpose(
                        lnps[0:1, 0:128], kn_bf[:, 0:1], self.ident[:]),
                        waits=[(V, vbf)], inc=self.es["tensor"])
                    v2_ = self.ed("vector", lambda E: E.reduce_max(
                        kn_sm[0:1, 2:3], lnps[0:1, 0:128],
                        axis=AX), waits=[(T, tt1)])
                    a2_ = self.ed("scalar", lambda E: E.activation(
                        kn_sm[0:1, 3:4], kn_sm[0:1, 2:3], AF.Sqrt),
                        waits=[(V, v2_)])
                    tb2 = self.ed("tensor", lambda E: E.matmul(
                        kps[:, 0:1], self.ones_row[:], kn_sm[0:1, 3:4],
                        start=True, stop=True), waits=[(A, a2_)])
                    vkf = self.ed("vector", lambda E: E.tensor_scalar_mul(
                        kn_sm[:, 2:3], kps[:, 0:1], -0.125),
                        waits=[(T, tb2)])
                    t_nq = 0
                    pv_sq = 0
                    for cdim in range(DC):
                        aq = self.ed("scalar", lambda E, cdim=cdim, sqi=sqi:
                                     E.activation(
                            sq2[:], qT[sqi][:, S * cdim:S * (cdim + 1)],
                            AF.Square),
                            waits=[(A, ev[(cdim + 1) * SQ - 1])]
                                  + ([(A, pv_sq)] if pv_sq else [])
                                  + ([(T, t_nq)] if t_nq else []))
                        pv_sq = aq
                        for nq in range(SQ):
                            t_nq = self.do(
                                "tensor",
                                lambda E, cdim=cdim, nq=nq: E.matmul(
                                    nqps[nq][:],
                                    self.ones16[:, NH * cdim:
                                                NH * (cdim + 1)],
                                    sq2[:, SW * nq:SW * (nq + 1)],
                                    start=(cdim == 0), stop=(cdim == DC - 1)),
                                waits=([(A, aq)] if nq == 0 else [])
                                      + ([(V, vkf)] if (cdim == 0 and
                                                        nq == 0) else []),
                                inc=T)
                    ev_nq = 0
                    for nq in range(SQ):
                        ev_nq = self.ed("vector", lambda E, nq=nq:
                                        E.tensor_copy(
                            qn2[:, SW * nq:SW * (nq + 1)], nqps[nq][:]),
                            waits=[(T, t_nq)] if nq == 0 else ())
                    t_tq = 0
                    for tb in range(QT):
                        t_tq = self.do("tensor", lambda E, tb=tb:
                                       E.transpose(
                            lnps[:, NH * tb:NH * (tb + 1)],
                            qn2[:, 128 * tb:128 * (tb + 1)],
                            self.ident[0:NH, 0:NH]),
                            waits=[(V, ev_nq), (V, vkf)] if tb == 0 else (),
                            inc=T if tb == QT - 1 else None)
                    vq2 = self.ed("vector", lambda E, sqi=sqi:
                                  E.tensor_copy(
                        biasC[sqi][:], lnps[:, 0:NH * QT]),
                        waits=[(T, t_tq)])
                    af = self.ed("scalar", lambda E, sqi=sqi: E.activation(
                        biasC[sqi][:], biasC[sqi][:], AF.Sqrt),
                        waits=[(V, vq2)])
                    fin_tick = self.ed("vector", lambda E, sqi=sqi:
                                       E.tensor_scalar_mul(
                        biasC[sqi][:], biasC[sqi][:], kn_sm[:, 2:3]),
                        waits=[(A, af)])
                    proj_fin[sqi] = [(A, ev[n0 + SQ - 1]),
                                     (V, evv[n1 + QT - 1]),
                                     (V, evk), (V, ones_t), (V, fin_tick)]
                    prev_seq_v = fin_tick
            with contextlib.ExitStack() as st2:
                en2 = st2.enter_context
                probs = en2(nc.sbuf_tensor("probsA", [128, 2 * S], BF16))
                probsT = en2(nc.sbuf_tensor("probsTA", [128, 2 * S], BF16))
                sc = en2(nc.psum_tensor("scA", [128, S], F32))
                psT = en2(nc.psum_tensor("psTA", [128, S], BF16))
                attps = [en2(nc.psum_tensor(f"attpsA{i}", [128, 65], F32))
                         for i in range(2)]

                def emit_scores(g, cq, w_):
                    sqi = g % 2
                    i = g // 2
                    h, qt = divmod(i, QT)
                    hj, hp = divmod(h, 2)
                    return self.do(
                        "tensor", lambda E, sqi=sqi, hj=hj, hp=hp,
                        qt=qt, cq=cq: E.matmul(
                            sc[:, SW * cq:SW * (cq + 1)],
                            qT[sqi][64 * hp:64 * hp + 64,
                                    S * hj + 128 * qt:
                                    S * hj + 128 * (qt + 1)],
                            kT[sqi][64 * hp:64 * hp + 64,
                                    SW * cq:SW * (cq + 1)],
                            start=True, stop=True),
                        waits=w_, inc=T)

                def emit_att(g, w, wmid=()):
                    sqi = g % 2
                    gb = g % 2
                    last = None
                    for cc in range(QT):
                        ww_ = (w if cc == 0 else
                               wmid if cc == QT // 2 else ())
                        last = self.do("tensor",
                                       lambda E, sqi=sqi, gb=gb, cc=cc:
                                       E.matmul(
                            attps[gb][:],
                            probsT[:, S * gb + 128 * cc:
                                   S * gb + 128 * (cc + 1)],
                            v_sb[sqi][:, 65 * cc:65 * (cc + 1)],
                            start=(cc == 0), stop=(cc == QT - 1)),
                            waits=ww_,
                            inc=T if cc == QT - 1 else None)
                    return last

                def x_slices(g):
                    sqi = g % 2
                    i = g // 2
                    jh, jqt = divmod(i, QT)
                    xsl = x_sb[:, D * (sqi * QT + jqt) + 64 * jh:
                               D * (sqi * QT + jqt) + 64 * (jh + 1)]
                    return [(xsl, attps[g % 2][:, 0:64])]

                first = {0: proj_fin[0] + proj_fin[1], 1: proj_fin[1]}

                def bias_src(g):
                    sqi = g % 2
                    i = g // 2
                    h, qt = divmod(i, QT)
                    return biasC[sqi][:, NH * qt + h:NH * qt + h + 1]

                self.attn_iters(2 * NH * QT, emit_scores, emit_att,
                                x_slices, probs, probsT, sc, psT, None,
                                0.125, first,
                                rec_src=lambda g: attps[g % 2][:, 64:65],
                                split_evac=(S >= 2048),
                                bias_src=bias_src, bias_tick=fin_tick)
                if post is not None:
                    post()
            if end_hook is not None:
                end_hook()
            self.barrier()
            self.flush()

    def attn_b(self, x_sb, bq, bk, vo, end_hook=None):
        c, nc = self.cfg, self.nc
        D, S, DC, QT = c["D"], c["S"], c["DC"], c["QT"]
        SQ, SW, OW, DH, NSEQ = c["SQ"], c["SW"], c["OW"], c["DH"], c["NSEQ"]
        V, A, T = self.es["vector"], self.es["scalar"], self.es["tensor"]
        assert NSEQ == 2
        dw = self.sem("dw")
        with contextlib.ExitStack() as st:
            en = st.enter_context
            xq = [en(nc.sbuf_tensor(f"xqB{q}", [128, S], BF16))
                  for q in range(2)]
            xk = [en(nc.sbuf_tensor(f"xkB{q}", [128, S], BF16))
                  for q in range(2)]
            xvo = [en(nc.sbuf_tensor(f"xvoB{q}", [128, QT * D], BF16))
                   for q in range(2)]
            ln_tmp = [en(nc.sbuf_tensor("lntmpB0", [128, D], BF16)),
                      self._hq0]
            proj_fin = {}
            with contextlib.ExitStack() as st1:
                en1 = st1.enter_context
                lnT = en1(nc.sbuf_tensor("lnT_B", [128, DC * S], BF16))
                vobh = [en1(nc.sbuf_tensor(f"vobB{i}", [128, DC * OW],
                                           BF16)) for i in range(2)]
                wqk = en1(nc.sbuf_tensor("wqkB", [128, DC * 128], BF16))
                t_vob = {}
                lnps = en1(nc.psum_tensor("lnpsB", [128, DC * 128], BF16))
                qps = [en1(nc.psum_tensor(f"qpsB{i}", [128, SW], F32))
                       for i in range(2)]
                ppw = []
                for nm in ("d_ppo0", "d_ppo1"):
                    if nm in self._sems:
                        sp = self._sems[nm]
                        ppw.append((sp, sp.n))
                if ppw:
                    self.do("gpsimd", lambda E: E.nop(), waits=ppw)
                for k in range(DC):
                    self.do("gpsimd", lambda E, k=k: E.dma_start(
                        wqk[:, 128 * k:128 * k + 64],
                        bq[128 * k:128 * (k + 1), :]), inc=dw, inck=16)
                    self.do("gpsimd", lambda E, k=k: E.dma_start(
                        wqk[:, 128 * k + 64:128 * (k + 1)],
                        bk[128 * k:128 * (k + 1), :]), inc=dw, inck=16)
                wt = dw.n
                pa4 = 0
                pev = {0: 0, 1: 0}
                ptr = {}
                alg_prev = 0
                prev_seq_v = 0
                for sqi in range(NSEQ):
                    tbase = sqi * QT
                    sb_ = 0
                    if sqi == 0 and self._pre is not None:
                        alg = self._pre
                        self._pre = None
                        sb_ = 64
                    else:
                        fw = [(V, alg_prev)] if alg_prev else []
                        lv, la = self.ln_stats(
                            list(range(tbase, tbase + QT)), x_sb, ln_tmp,
                            w_first=fw)
                        alg = self.ln_alg(QT, lv, la)
                    alg_prev = alg
                    ln_done = 0
                    lnpp = [lnps[:],
                            qps[0][:].bitcast(BF16)[:, 0:DC * 128]]
                    for tl in range(QT):
                        vn, a4, ttk = self.ln_norm_tr(
                            tbase + tl, tl, x_sb, ln_tmp,
                            lnpp[tl % 2], lnT, S,
                            128 * tl, alg, prev_tt=ptr.get(tl - 2, 0),
                            w_tr=([(A, pev[tl % 2])]
                                  if pev[tl % 2] else ()), sb=sb_)
                        ptr[tl] = ttk
                        pev[tl % 2] = a4
                        ln_done = a4
                    ptr = {}
                    ev = {}
                    evd = {}
                    for which in range(2):
                        dst = xq[sqi] if which == 0 else xk[sqi]
                        col0 = 0 if which == 0 else 64
                        scl = 1.0 if which == 0 else 1.0 / float(D) ** 0.5
                        for nq in range(SQ):
                            idx = which * SQ + nq
                            pb = idx % 2
                            last = None
                            for k in range(DC):
                                wl = []
                                if idx == 0 and k == 0:
                                    wl = [(dw, wt), (A, ln_done)]
                                    if prev_seq_v:
                                        wl.append((V, prev_seq_v))
                                if k == 0 and idx >= 2:
                                    wl = wl + [(A, ev[idx - 2])]
                                last = self.do(
                                    "tensor",
                                    lambda E, nq=nq, k=k, pb=pb, col0=col0:
                                    E.matmul(
                                        qps[pb][0:64, :],
                                        wqk[:, 128 * k + col0:
                                            128 * k + col0 + 64],
                                        lnT[:, S * k + SW * nq:
                                            S * k + SW * (nq + 1)],
                                        start=(k == 0), stop=(k == DC - 1)),
                                    waits=wl,
                                    inc=T if k == DC - 1 else None)
                            ev[idx] = self.ed(
                                "scalar", lambda E, nq=nq, pb=pb, dst=dst,
                                scl=scl: E.mul(
                                    dst[0:64, SW * nq:SW * (nq + 1)],
                                    qps[pb][0:64, :], scl),
                                waits=[(T, last)])
                            evd[idx] = self.ed(
                                "vector", lambda E, nq=nq, dst=dst:
                                E.tensor_copy(
                                    dst[64:128, SW * nq:SW * (nq + 1)],
                                    dst[0:64, SW * nq:SW * (nq + 1)]),
                                waits=[(A, ev[idx])])
                    evv = {}
                    n0 = 2 * SQ
                    for dh in range(DH):
                        vp = (sqi * DH + dh) % 2
                        dsem = self.sem(f"d_vo{vp}")
                        for k in range(DC):
                            self.do("gpsimd", lambda E, k=k, dh=dh, vp=vp:
                                    E.dma_start(
                                vobh[vp][:, OW * k:OW * (k + 1)],
                                vo[128 * k:128 * (k + 1),
                                   OW * dh:OW * (dh + 1)]),
                                waits=([(T, t_vob.get(vp, 0))]
                                       if (k == 0 and t_vob.get(vp, 0))
                                       else []), inc=dsem, inck=16)
                        wtv = dsem.n
                        for tb in range(QT):
                            idx = n0 + dh * QT + tb
                            pb = idx % 2
                            last = None
                            for k in range(DC):
                                wl = []
                                if k == 0:
                                    if tb == 0:
                                        wl = [(dsem, wtv)]
                                    if (idx - 2) in evv:
                                        wl = wl + [(V, evv[idx - 2])]
                                    elif (idx - 2) in ev:
                                        wl = wl + [(A, ev[idx - 2])]
                                last = self.do(
                                    "tensor",
                                    lambda E, tb=tb, dh=dh, k=k, pb=pb,
                                    vp=vp: E.matmul(
                                        qps[pb][:, 0:OW],
                                        lnT[:, S * k + 128 * tb:
                                            S * k + 128 * (tb + 1)],
                                        vobh[vp][:, OW * k:OW * (k + 1)],
                                        start=(k == 0), stop=(k == DC - 1)),
                                    waits=wl,
                                    inc=T if k == DC - 1 else None)
                            evv[idx] = self.ed(
                                "vector", lambda E, tb=tb, dh=dh, pb=pb,
                                sqi=sqi: E.tensor_copy(
                                    xvo[sqi][:, D * tb + OW * dh:
                                             D * tb + OW * (dh + 1)],
                                    qps[pb][:, 0:OW]), waits=[(T, last)])
                        t_vob[vp] = last
                    proj_fin[sqi] = [(A, ev[2 * SQ - 1]),
                                     (V, evv[n0 + QT * DH - 1]),
                                     (V, evd[2 * SQ - 1])]
                    prev_seq_v = evv[n0 + QT * DH - 1]
            with contextlib.ExitStack() as st2:
                en2 = st2.enter_context
                probs = en2(nc.sbuf_tensor("probsB", [128, 2 * S], BF16))
                probsT = en2(nc.sbuf_tensor("probsTB", [128, 2 * S], BF16))
                sc = en2(nc.psum_tensor("scB", [128, S], F32))
                psT = en2(nc.psum_tensor("psTB", [128, S], BF16))
                attps = en2(nc.psum_tensor("attpsB", [128, D], F32))

                def emit_scores(g, cq, w_):
                    sqi = g % 2
                    i = g // 2
                    qt = i
                    hp = i % 2
                    return self.do(
                        "tensor", lambda E, sqi=sqi, qt=qt, hp=hp,
                        cq=cq: E.matmul(
                            sc[:, SW * cq:SW * (cq + 1)],
                            xq[sqi][64 * hp:64 * hp + 64,
                                    128 * qt:128 * (qt + 1)],
                            xk[sqi][64 * hp:64 * hp + 64,
                                    SW * cq:SW * (cq + 1)],
                            start=True, stop=True),
                        waits=w_, inc=T)

                def emit_att(g, w, wmid=()):
                    sqi = g % 2
                    gb = g % 2
                    w = list(w)
                    if g >= 1:
                        w.append((V, self._t_stt_cur.get(g - 1, 0)))
                    last = None
                    nmm = 0
                    for cc in range(QT):
                        for dh in range(DH):
                            nmm += 1
                            ww_ = (w if nmm == 1 else
                                   wmid if (cc == QT // 2 and dh == 0)
                                   else ())
                            last = self.do(
                                "tensor", lambda E, sqi=sqi, gb=gb, cc=cc,
                                dh=dh: E.matmul(
                                    attps[:, OW * dh:OW * (dh + 1)],
                                    probsT[:, S * gb + 128 * cc:
                                           S * gb + 128 * (cc + 1)],
                                    xvo[sqi][:, D * cc + OW * dh:
                                             D * cc + OW * (dh + 1)],
                                    start=(cc == 0), stop=(cc == QT - 1)),
                                waits=ww_,
                                inc=T if nmm == QT * DH else None)
                    return last

                def x_slices(g):
                    sqi = g % 2
                    qt = g // 2
                    out = []
                    for dh in range(DH):
                        xsl = x_sb[:, D * (sqi * QT + qt) + OW * dh:
                                   D * (sqi * QT + qt) + OW * (dh + 1)]
                        out.append((xsl, attps[:, OW * dh:OW * (dh + 1)]))
                    return out

                first = {0: proj_fin[0] + proj_fin[1], 1: proj_fin[1]}
                self.attn_iters(2 * QT, emit_scores, emit_att, x_slices,
                                probs, probsT, sc, psT, None, 1.0, first,
                                split_evac=(S >= 2048))
            self.barrier()
            self.flush()

    def ffn(self, x_sb, w0s, w1s, w2s, yout, hooks=None):
        c, nc = self.cfg, self.nc
        D, DC, FC, TT, NTT, TG = (c["D"], c["DC"], c["FC"], c["TT"],
                                  c["NTT"], c["TG"])
        OW, DH = c["OW"], c["DH"]
        V, A, T = self.es["vector"], self.es["scalar"], self.es["tensor"]
        MG = min(4, FC)
        NMG = FC // MG
        KQ = min(8, FC)
        NKQ = FC // KQ
        final = yout is not None
        tag = "g" if final else "f"
        with contextlib.ExitStack() as st:
            en = st.enter_context
            ln_tmp = [en(nc.sbuf_tensor(f"{tag}_lntmp{i}", [128, D], BF16))
                      for i in range(2)]
            sq_tmp = [en(nc.sbuf_tensor(f"{tag}_lnsq{i}", [128, D], BF16))
                      for i in range(2)]
            lnT = en(nc.sbuf_tensor(f"{tag}_lnT", [128, DC * TT], BF16))
            h1 = en(nc.sbuf_tensor(f"{tag}_h1", [128, FC * TT], BF16))
            h2 = en(nc.sbuf_tensor(f"{tag}_h2", [128, FC * TT], BF16))
            w0b = [en(nc.sbuf_tensor(f"{tag}_w0{i}", [128, DC * 512], BF16))
                   for i in range(2)]
            w1b = [en(nc.sbuf_tensor(f"{tag}_w1{i}",
                                     [128, KQ * 128 * min(2, FC)], BF16))
                   for i in range(4)]
            w2b = [en(nc.sbuf_tensor(f"{tag}_w2{i}", [128, D], BF16))
                   for i in range(4)]
            stg = ([en(nc.sbuf_tensor(f"{tag}_st{i}", [128, OW], F32))
                    for i in range(4)] if final else None)
            h1ps = [en(nc.psum_tensor(f"{tag}_h1ps{i}", [128, TT], F32))
                    for i in range(2)]
            h2ps = [en(nc.psum_tensor(f"{tag}_h2ps{i}", [128, TT], F32))
                    for i in range(MG)]
            outps = en(nc.psum_tensor(f"{tag}_outps", [128, 1024], F32))
            d0 = [self.sem(f"d_{tag}0{i}") for i in range(2)]
            d1 = [self.sem(f"d_{tag}1{i}") for i in range(4)]
            d2 = [self.sem(f"d_{tag}2{i}") for i in range(4)]
            dy = ([self.sem(f"d_y{i}") for i in range(4)]
                  if final else None)
            assert DC * 128 * 2 <= TT * 4

            t_w0 = {0: 0, 1: 0}
            t_w1 = {0: 0, 1: 0, 2: 0, 3: 0}
            t_w2 = {0: 0, 1: 0, 2: 0, 3: 0}
            t_stg = [0, 0, 0, 0]
            t_h2evac = 0
            t_add = 0
            lnview = h2ps[0][:].bitcast(BF16)
            state = {"ptr": {}, "pa4": 0, "t_h2evac": 0}

            def emit_stats(tt):
                tiles = [(TT * tt) // 128 + tg for tg in range(TG)]
                lv, la = self.ln_stats(tiles, x_sb, sq_tmp)
                return self.ln_alg(TG, lv, la)

            def emit_norm(tt, alg, sb=0):
                ld = 0
                for tg in range(TG):
                    tglob = (TT * tt) // 128 + tg
                    wtr = [(A, state["pa4"])] if state["pa4"] else []
                    if tg == 0 and state["t_h2evac"]:
                        wtr.append((A, state["t_h2evac"]))
                    vn, a4, ttk = self.ln_norm_tr(
                        tglob, tg, x_sb, ln_tmp, lnview, lnT, TT, 128 * tg,
                        alg, prev_tt=state["ptr"].get(tg - 2, 0), w_tr=wtr,
                        sb=sb)
                    state["ptr"][tg] = ttk
                    state["pa4"] = a4
                    ld = a4
                return ld

            if self._pre is not None:
                next_ln = emit_norm(0, self._pre, sb=64)
                self._pre = None
            else:
                next_ln = emit_norm(0, emit_stats(0))
            alg_next = None
            for tt in range(NTT):
                ln_done = next_ln
                ev_h1 = {}
                for mg in range(NMG):
                    ib = mg % 2
                    self.do("sync", lambda E, mg=mg, ib=ib: E.dma_start(
                        w0b[ib][:].rearrange("p (k w) -> p k w", k=DC),
                        w0s[:, 512 * mg:512 * (mg + 1)]
                        .rearrange("(k p) w -> p k w", p=128)),
                        waits=([(T, t_w0[ib])] if t_w0[ib]
                               else list(getattr(self, "_pp_w0", []))),
                        inc=d0[ib], inck=16)
                    for m4 in range(MG):
                        m = MG * mg + m4
                        pb = m % 2
                        last = None
                        for k in range(DC):
                            wl = []
                            if m4 == 0 and k == 0:
                                wl = [(d0[ib], d0[ib].n), (A, ln_done)]
                                if mg == 0 and t_add:
                                    wl.append((V, t_add))
                            if k == 0 and m >= 2:
                                wl = wl + [(A, ev_h1[m - 2])]
                            last = self.do(
                                "tensor",
                                lambda E, m4=m4, k=k, ib=ib, pb=pb:
                                E.matmul(
                                    h1ps[pb][:],
                                    w0b[ib][:, 512 * k + 128 * m4:
                                            512 * k + 128 * (m4 + 1)],
                                    lnT[:, TT * k:TT * (k + 1)],
                                    start=(k == 0), stop=(k == DC - 1)),
                                waits=wl, inc=T if k == DC - 1 else None)
                        if m4 == MG - 1:
                            t_w0[ib] = last
                        ev_h1[m] = self.ed("scalar", lambda E, m=m, pb=pb:
                                           E.activation(
                            h1[:, TT * m:TT * (m + 1)], h1ps[pb][:],
                            AF.Relu), waits=[(T, last)])
                if tt + 1 < NTT:
                    alg_next = emit_stats(tt + 1)
                ev_h2 = {}
                M1G = min(2, FC)
                N1G = FC // M1G
                for mg in range(N1G):
                    pr = mg % 2
                    last_mm = None
                    for q in range(NKQ):
                        ib = (mg * NKQ + q) % 4
                        self.do("sync", lambda E, mg=mg, q=q, ib=ib,
                                M1G=M1G: E.dma_start(
                            w1b[ib][:, 0:KQ * 128 * M1G]
                            .rearrange("p (k w) -> p k w", k=KQ),
                            w1s[KQ * 128 * q:KQ * 128 * (q + 1),
                                128 * M1G * mg:128 * M1G * (mg + 1)]
                            .rearrange("(k p) w -> p k w", p=128)),
                            waits=([(T, t_w1[ib])] if t_w1[ib]
                                   else list(getattr(self, "_pp_w1", []))),
                            inc=d1[ib], inck=16)
                        for k8 in range(KQ):
                            for m4 in range(M1G):
                                wl = []
                                if k8 == 0 and m4 == 0:
                                    wl = [(d1[ib], d1[ib].n)]
                                    if mg <= 1:
                                        wl.append(
                                            (A, ev_h1[min(KQ * (q + 1),
                                                          FC) - 1]))
                                    elif q == 0:
                                        wl.append(
                                            (A, ev_h2[M1G * (mg - 1) - 1]))
                                last_mm = self.do(
                                    "tensor",
                                    lambda E, q=q, k8=k8, m4=m4, ib=ib,
                                    pr=pr, M1G=M1G: E.matmul(
                                        h2ps[2 * pr + m4][:],
                                        w1b[ib][:, 128 * M1G * k8 +
                                                128 * m4:
                                                128 * M1G * k8 +
                                                128 * (m4 + 1)],
                                        h1[:, TT * (KQ * q + k8):
                                           TT * (KQ * q + k8 + 1)],
                                        start=(q == 0 and k8 == 0),
                                        stop=(q == NKQ - 1 and
                                              k8 == KQ - 1)),
                                    waits=wl,
                                    inc=T if (k8 == KQ - 1 and
                                              m4 == M1G - 1) else None)
                        t_w1[ib] = last_mm
                    for m4 in range(M1G):
                        m = M1G * mg + m4
                        ev_h2[m] = self.ed("scalar",
                                           lambda E, m=m, m4=m4, pr=pr:
                                           E.activation(
                            h2[:, TT * m:TT * (m + 1)],
                            h2ps[2 * pr + m4][:],
                            AF.Relu), waits=[(T, last_mm)])
                t_h2evac = ev_h2[FC - 1]
                state["t_h2evac"] = t_h2evac
                if tt + 1 < NTT:
                    next_ln = emit_norm(tt + 1, alg_next)
                accs4 = ([outps[:, 0:OW], outps[:, 512:512 + OW]]
                         + [h1ps[i][:, 0:OW] for i in range(2)])
                PT = max(1, 4 // DH)
                for p0 in range(0, TG, PT):
                    tgs = list(range(p0, min(p0 + PT, TG)))
                    last_mm = None
                    for k in range(FC):
                        ib = k % 4
                        self.do("sync", lambda E, k=k, ib=ib: E.dma_start(
                            w2b[ib][:], w2s[128 * k:128 * (k + 1), :]),
                            waits=([(T, t_w2[ib])] if t_w2[ib]
                                   else list(getattr(self, "_pp_w2", []))),
                            inc=d2[ib], inck=16)
                        for ti, tg in enumerate(tgs):
                            for dh in range(DH):
                                wl = []
                                if ti == 0 and dh == 0:
                                    wl = [(d2[ib], d2[ib].n)]
                                    if k == 0:
                                        wl += [(A, t_h2evac), (V, t_add)]
                                acc = accs4[ti * DH + dh]
                                last_mm = self.do(
                                    "tensor",
                                    lambda E, k=k, tg=tg, dh=dh, ib=ib,
                                    acc=acc: E.matmul(
                                        acc,
                                        h2[:, TT * k + 128 * tg:
                                           TT * k + 128 * (tg + 1)],
                                        w2b[ib][:, OW * dh:OW * (dh + 1)],
                                        start=(k == 0), stop=(k == FC - 1)),
                                    waits=wl,
                                    inc=T if (ti == len(tgs) - 1 and
                                              dh == DH - 1) else None)
                        t_w2[ib] = last_mm
                    for ti, tg in enumerate(tgs):
                        tglob = (TT * tt) // 128 + tg
                        for dh in range(DH):
                            xsl = x_sb[:, D * tglob + OW * dh:
                                       D * tglob + OW * (dh + 1)]
                            src = accs4[ti * DH + dh]
                            if final:
                                sb = (ti * DH + dh) % 4
                                t_add = self.ed(
                                    "vector",
                                    lambda E, xsl=xsl, src=src, sb=sb:
                                    E.tensor_add(stg[sb][:], xsl, src),
                                    waits=[(T, last_mm)]
                                    + ([(dy[sb], t_stg[sb])] if t_stg[sb]
                                       else []))
                                t_stg[sb] = self.do(
                                    "scalar",
                                    lambda E, tglob=tglob, dh=dh, sb=sb:
                                    E.dma_start(
                                        yout[128 * tglob:
                                             128 * (tglob + 1),
                                             OW * dh:OW * (dh + 1)],
                                        stg[sb][:]),
                                    waits=[(V, t_add)], inc=dy[sb], inck=16)
                            else:
                                t_add = self.ed(
                                    "vector", lambda E, xsl=xsl, src=src:
                                    E.tensor_add(xsl, xsl, src),
                                    waits=[(T, last_mm)])
                if hooks and tt in hooks:
                    hooks[tt]()
            if final:
                self.do("scalar", lambda E: E.nop(),
                        waits=[(d, d.n) for d in dy])
            self.barrier()
            self.flush()


def build_mk(cfg):
    return MK(cfg).build()


def host_prep(inputs, n_cores=8):
    x = np.ascontiguousarray(np.asarray(inputs["L0_attn_a_x"]))
    B = x.shape[0]
    per = B // n_cores
    shared = dict(
        wq=np.ascontiguousarray(np.asarray(inputs["L0_attn_a_Wq"]).T),
        wk=np.ascontiguousarray(np.asarray(inputs["L0_attn_a_Wk"]).T),
        wv=np.ascontiguousarray(np.asarray(inputs["L0_attn_a_Wv"]).T),
        bq=np.ascontiguousarray(np.asarray(inputs["L0_attn_b_Q"]).T),
        bk=np.ascontiguousarray(np.asarray(inputs["L0_attn_b_K"]).T),
        vo=np.ascontiguousarray(np.asarray(inputs["L0_attn_b_VO"])),
        k0=np.ascontiguousarray(np.asarray(inputs["L0_ffn_k0"])).view(np.uint8),
        k1=np.ascontiguousarray(np.asarray(inputs["L0_ffn_k1"])).view(np.uint8),
        k2=np.ascontiguousarray(np.asarray(inputs["L0_ffn_k2"])).view(np.uint8),
        s0=np.asarray(inputs["L0_ffn_s0"], dtype=np.float32).reshape(1, 1),
        s1=np.asarray(inputs["L0_ffn_s1"], dtype=np.float32).reshape(1, 1),
        s2=np.asarray(inputs["L0_ffn_s2"], dtype=np.float32).reshape(1, 1),
    )
    in_maps = []
    for cidx in range(n_cores):
        m = dict(shared)
        m["xin"] = np.ascontiguousarray(
            x[cidx * per:(cidx + 1) * per].reshape(-1, x.shape[2]))
        in_maps.append(m)
    return in_maps


def kernel(**inputs):
    cfg = REAL_CFG
    n_cores = 8
    nc = build_mk(cfg)
    in_maps = host_prep(inputs, n_cores)
    res = run_bass_kernel_spmd(nc, in_maps, core_ids=list(range(n_cores)))
    B, S, D = np.asarray(inputs["L0_attn_a_x"]).shape
    per = B // n_cores
    out = np.concatenate(
        [np.asarray(r["y"]).reshape(per, S, D) for r in res.results], axis=0)
    return out.astype(np.float32)
